# version 01
# baseline (speedup 1.0000x reference)
from contextlib import ExitStack

import numpy as np
import concourse.bass as bass
import concourse.mybir as mybir
from concourse.bass_utils import run_bass_kernel_spmd

F32 = mybir.dt.float32
F32R = mybir.dt.float32r
BF16 = mybir.dt.bfloat16
AF = mybir.ActivationFunctionType
ALU = mybir.AluOpType

B, L, DM = 16, 2048, 1024
DI, DS, DR, DC, NL = 2048, 16, 64, 4, 4
DH = 4096
NCORES, SPC = 8, 2
NDB = DM // 128
NIB = DI // 128
NHB = DH // 128
TC = 256
NCH = L // TC
TH = 1024
TF = 512
NFT = L // TF
EPS = 1e-5
GD = 4
NG = NIB // GD

ENGS = ("tensor", "vector", "scalar", "gpsimd", "sync")
AUDIT = {}


class SemT:
    def __init__(self, h, name):
        self.h, self.n, self.name = h, 0, name


def _is_ap(x):
    return hasattr(x, "tensor") and hasattr(x, "offset") and hasattr(x, "ap")


def _ap_access(x, is_w, sbuf_addr):
    t = x.tensor
    sp = str(t.space)
    pairs = [list(p) for p in x.ap]
    esz = {"dt.float32": 4, "dt.float32r": 4, "dt.bfloat16": 2}.get(
        str(x.dtype), 4)
    if "SBUF" in sp:
        rowel = int(t.shape[1]) if len(t.shape) > 1 else 1
        off = int(x.offset)
        col = off % rowel
        span = 1 + sum((c - 1) * abs(s) for s, c in pairs[1:])
        base = sbuf_addr.get(t.name, (0,))[0]
        return ("SBUF", "all", base + col * esz, base + (col + span) * esz,
                is_w)
    if "PSUM" in sp:
        return ("PSUM", t.name, 0, 1, is_w)
    off = int(x.offset)
    span = 1 + sum((c - 1) * abs(s) for s, c in pairs)
    return ("DRAM", t.name, off * esz, (off + span) * esz, is_w)


class Rec:
    def __init__(self, sems, serial=False):
        self.sched = {e: [] for e in ENGS}
        self.lastw = {e: {} for e in ENGS}
        self.sems = sems
        self.nbar = 0
        self.serial = serial
        self.stage = "pro"
        self.audit = None
        self.sbuf_addr = {}
        self.ntag = 0

    def _waits(self, eng, waits):
        out = []
        for m in waits:
            if m is None:
                continue
            sem, val = m
            if self.lastw[eng].get(sem, -1) >= val:
                continue
            self.lastw[eng][sem] = val
            out.append((sem, val))
        return out

    def emit(self, _eng, _opname, *args, waits=(), inc=None, incn=1, **kw):
        eng = _eng
        if self.serial and (self.serial is True or eng in self.serial
                            or f"{eng}@{self.stage}" in self.serial):
            waits = list(waits) + [(s, s.n) for s in self.sems.values()
                                   if s.name != "bar"]
        wl = self._waits(eng, waits)
        marker = None
        if inc is not None:
            if inc.name in ("dS", "dG"):
                key = inc.name
                pool = self.pools[key]
                inc = pool[self.rr[key] % len(pool)]
                self.rr[key] += 1
            inc.n += incn
            marker = (inc, inc.n)
        if self.audit is not None:
            self.ntag += 1
            accesses = []
            if _opname in ("memset", "matmul"):
                accesses.append(_ap_access(args[0], True, self.sbuf_addr))
            for k, v in kw.items():
                if _is_ap(v):
                    accesses.append(_ap_access(v, k.startswith("out"),
                                               self.sbuf_addr))
            self.audit.append({
                "kind": "dma" if _opname == "dma_start" else "ins",
                "eng": eng,
                "waits": [(s.name, v) for s, v in wl],
                "inc": (inc.name, inc.n, incn) if inc is not None else None,
                "accesses": accesses,
                "tag": f"{self.stage}#{_opname}#{self.ntag}",
            })

        def f(e, wl=wl, opname=_opname, args=args, kw=kw, inc=inc, incn=incn):
            for s, v in wl:
                e.wait_ge(s.h, v)
            ins = getattr(e, opname)(*args, **kw)
            if inc is not None:
                ins.then_inc(inc.h, incn)

        self.sched[eng].append(f)
        return marker

    def now(self, sem):
        return (sem, sem.n)

    def barrier(self):
        bar = self.sems["bar"]
        self.nbar += 1
        tgt = 5 * self.nbar
        for eng in ENGS:
            if eng == "sync":
                dw = self._waits(eng, [(s, s.n) for s in self.pools["dS"]])
            elif eng == "gpsimd":
                dw = self._waits(eng, [(s, s.n) for s in self.pools["dG"]])
            else:
                dw = []

            def f(e, dw=dw, bar=bar, tgt=tgt):
                for s, v in dw:
                    e.wait_ge(s.h, v)
                e.sem_inc(bar.h, 1)
                e.wait_ge(bar.h, tgt)

            self.sched[eng].append(f)
            self.lastw[eng][bar] = tgt
        bar.n = tgt
        if self.audit is not None:
            self.audit.append({"kind": "barrier"})

    def finish(self):
        ds_list = [(s, s.n) for s in self.pools["dS"]]
        dg_list = [(s, s.n) for s in self.pools["dG"]]

        def fs(e, lst=ds_list):
            for s, v in lst:
                e.wait_ge(s.h, v)

        def fg(e, lst=dg_list):
            for s, v in lst:
                e.wait_ge(s.h, v)

        self.sched["sync"].append(fs)
        self.sched["gpsimd"].append(fg)


def _prep_host(inp):
    d = {}
    x = np.asarray(inp["L0_attn_input_ids"], dtype=np.float32)
    d["x_fm"] = np.ascontiguousarray(x.transpose(0, 2, 1))

    d["w_inT"] = np.ascontiguousarray(
        np.asarray(inp["L0_attn_in_proj_w"], np.float32).transpose(0, 2, 1))
    d["w_xpT"] = np.ascontiguousarray(
        np.asarray(inp["L0_attn_x_proj_w"], np.float32).transpose(0, 2, 1))
    d["w_dtpT"] = np.ascontiguousarray(
        np.asarray(inp["L0_attn_dt_proj_w"], np.float32).transpose(0, 2, 1))
    d["w_outT"] = np.ascontiguousarray(
        np.asarray(inp["L0_attn_out_proj_w"], np.float32).transpose(0, 2, 1))
    d["w_f1T"] = np.ascontiguousarray(np.asarray(inp["L0_ffn_w1_w"], np.float32).T)
    d["w_f2T"] = np.ascontiguousarray(np.asarray(inp["L0_ffn_w2_w"], np.float32).T)
    d["w_h1T"] = np.ascontiguousarray(np.asarray(inp["L0_ffn_h1_w"], np.float32).T)
    d["w_h2T"] = np.ascontiguousarray(np.asarray(inp["L0_ffn_h2_w"], np.float32).T)
    d["w_f3T"] = np.ascontiguousarray(np.asarray(inp["L0_ffn_w3_w"], np.float32).T)

    def p128(a, last):
        a = np.asarray(a, np.float32)
        return np.ascontiguousarray(a.reshape(a.shape[0], last, 128).transpose(0, 2, 1))

    d["norm_w"] = p128(inp["L0_attn_norm_w"], NDB)
    d["norm_b"] = p128(inp["L0_attn_norm_b"], NDB)
    d["conv_b"] = p128(inp["L0_attn_conv_b"], NIB)
    d["dtp_b"] = p128(inp["L0_attn_dt_proj_b"], NIB)
    d["Dp"] = p128(inp["L0_attn_D"], NIB)
    cw = np.asarray(inp["L0_attn_conv_w"], np.float32)
    d["conv_w"] = np.ascontiguousarray(
        cw.reshape(NL, NIB, 128, DC).transpose(0, 2, 1, 3).reshape(NL, 128, NIB * DC))
    al = np.asarray(inp["L0_attn_A_log"], np.float32)
    d["A_log"] = np.ascontiguousarray(
        al.reshape(NL, NIB, 128, DS).transpose(0, 2, 1, 3).reshape(NL, 128, NIB * DS))
    d["normf_w"] = np.ascontiguousarray(
        np.asarray(inp["L0_attn_normf_w"], np.float32).reshape(NDB, 128).T)
    d["normf_b"] = np.ascontiguousarray(
        np.asarray(inp["L0_attn_normf_b"], np.float32).reshape(NDB, 128).T)

    def s128(a):
        return np.ascontiguousarray(np.asarray(a, np.float32).reshape(-1, 128).T)

    d["f1s"] = s128(inp["L0_ffn_w1_s"])
    d["f2s"] = s128(inp["L0_ffn_w2_s"])
    d["h1s"] = s128(inp["L0_ffn_h1_s"])
    d["h2s"] = s128(inp["L0_ffn_h2_s"])
    d["f3s"] = s128(inp["L0_ffn_w3_s"])

    d["sel"] = np.ascontiguousarray(
        np.repeat(np.eye(DS, dtype=np.float32)[:, :, None], 128, axis=2)
        .transpose(1, 0, 2).reshape(DS, DS * 128))
    d["ones128"] = np.full((128, 1), 1.0 / DM, np.float32)
    return d


def build_program(dbg=False, serial=False, audit=False):
    nc = bass.Bass("TRN2", target_bir_lowering=False)

    def din(name, shape, dt):
        return nc.dram_tensor(name, list(shape), dt, kind="ExternalInput").ap()

    x_fm = din("x_fm", (SPC, DM, L), F32)
    w_inT = din("w_inT", (NL, DM, 2 * DI), F32R)
    w_xpT = din("w_xpT", (NL, DI, 96), F32R)
    w_dtpT = din("w_dtpT", (NL, DR, DI), F32R)
    w_outT = din("w_outT", (NL, DI, DM), F32R)
    w_f1T = din("w_f1T", (DM, DH), F32)
    w_f2T = din("w_f2T", (DM, DH), F32)
    w_h1T = din("w_h1T", (DH, DH), F32)
    w_h2T = din("w_h2T", (DH, DH), F32)
    w_f3T = din("w_f3T", (DH, DM), F32)
    norm_w = din("norm_w", (NL, 128, NDB), F32)
    norm_b = din("norm_b", (NL, 128, NDB), F32)
    conv_w = din("conv_w", (NL, 128, NIB * DC), F32)
    conv_b = din("conv_b", (NL, 128, NIB), F32)
    dtp_b = din("dtp_b", (NL, 128, NIB), F32)
    Dp = din("Dp", (NL, 128, NIB), F32)
    A_log = din("A_log", (NL, 128, NIB * DS), F32)
    normf_w = din("normf_w", (128, NDB), F32)
    normf_b = din("normf_b", (128, NDB), F32)
    f1s = din("f1s", (128, NHB), F32)
    f2s = din("f2s", (128, NHB), F32)
    h1s = din("h1s", (128, NHB), F32)
    h2s = din("h2s", (128, NHB), F32)
    f3s = din("f3s", (128, NDB), F32)
    sel = din("sel", (DS, DS * 128), F32R)
    ones128 = din("ones128", (128, 1), F32)

    out_fm = nc.dram_tensor("out_fm", [SPC, DM, L], F32, kind="ExternalOutput").ap()

    def scr(name, shape, dt):
        return nc.dram_tensor(name, list(shape), dt, kind="Internal").ap()

    sp_xn_r = scr("sp_xn_r", (NDB, 128, L), F32)
    sp_xn_b = scr("sp_xn_b", (NDB, 128, L), BF16)
    sp_r = scr("sp_r", (NDB, 128, L), F32)
    sp_z = scr("sp_z", (NIB, 128, L), BF16)
    sp_xc = scr("sp_xc", (NIB, 128, L), F32R)
    sp_dtr = scr("sp_dtr", (NIB, 128, L), F32)
    sp_y = scr("sp_y", (NIB, 128, L), F32)
    sp_hg3 = scr("sp_hg3", (NHB, 128, L), BF16)

    dbg_t = {}
    es = ExitStack()

    _uid = [0]
    _sb_init = [None]

    def sb(name, shape, dt, st=None):
        stack = st if st is not None else es
        _uid[0] += 1
        if _sb_init[0] is None:
            _sb_init[0] = nc.sbuf_bytes_remaining
        before = nc.sbuf_bytes_remaining
        h = stack.enter_context(
            nc.sbuf_tensor(f"{name}_{_uid[0]}", list(shape), dt))
        after = nc.sbuf_bytes_remaining
        start = _sb_init[0] - before
        R.sbuf_addr[h.name] = (start, start + (before - after))
        return h

    sems = {}
    for sn in ("dS", "dG", "sP", "sA", "sV", "sO", "bar"):
        sems[sn] = SemT(es.enter_context(nc.semaphore(sn)), sn)
    for i in range(8):
        sems[f"dSp{i}"] = SemT(es.enter_context(nc.semaphore(f"dSp{i}")),
                               f"dSp{i}")
    for i in range(6):
        sems[f"dGp{i}"] = SemT(es.enter_context(nc.semaphore(f"dGp{i}")),
                               f"dGp{i}")
    dS, dG, sP, sA, sV, sO = (sems[k] for k in ("dS", "dG", "sP", "sA", "sV", "sO"))
    R = Rec(sems, serial=serial)
    R.pools = {"dS": [sems[f"dSp{i}"] for i in range(8)],
               "dG": [sems[f"dGp{i}"] for i in range(6)]}
    R.rr = {"dS": 0, "dG": 0}
    if audit:
        AUDIT["acc"] = R.audit = []

    def dbg_dump(tag, src_ap, shape, dt, lvl=1):
        if not dbg or lvl > int(dbg):
            return
        t = nc.dram_tensor(f"dbg_{tag}", list(shape), dt,
                           kind="ExternalOutput").ap()
        dbg_t[tag] = t
        R.emit("sync", "dma_start", out=t, in_=src_ap, inc=dS, incn=16)
        R.barrier()

    p_nw = sb("p_nw", (128, NL * NDB), F32)
    p_nb = sb("p_nb", (128, NL * NDB), F32)
    p_cw = sb("p_cw", (128, NL * NIB * DC), F32)
    p_cb = sb("p_cb", (128, NL * NIB), F32)
    p_dtb = sb("p_dtb", (128, NL * NIB), F32)
    p_D = sb("p_D", (128, NL * NIB), F32)
    p_An = sb("p_An", (128, NL * NIB * DS), F32)
    p_nfw = sb("p_nfw", (128, NDB), F32)
    p_nfb = sb("p_nfb", (128, NDB), F32)
    p_f1s = sb("p_f1s", (128, NHB), F32)
    p_f2s = sb("p_f2s", (128, NHB), F32)
    p_h1s = sb("p_h1s", (128, NHB), F32)
    p_h2s = sb("p_h2s", (128, NHB), F32)
    p_f3s = sb("p_f3s", (128, NDB), F32)
    p_sel = sb("p_sel", (DS, DS * 128), F32R)
    p_o128 = sb("p_o128", (128, 1), F32)
    Bc = sb("Bc", (DS, L), F32R)
    Cc = sb("Cc", (DS, L), F32R)
    lnrow = sb("lnrow", (1, 4 * TC), F32)
    lnrow_r = sb("lnrow_r", (1, 2 * TC), F32R)
    lnsq = sb("lnsq", (128, TC), F32)
    p_eps = sb("p_eps", (1, 1), F32)
    carry = sb("carry", (128, GD * DS), F32)

    pbk = [es.enter_context(nc.psum_tensor(f"pb{i}", [128, 512], F32))
           for i in range(8)]

    for dt_t, src in ((p_nw, norm_w), (p_nb, norm_b), (p_cb, conv_b),
                      (p_dtb, dtp_b), (p_D, Dp), (p_nfw, normf_w),
                      (p_nfb, normf_b), (p_f1s, f1s), (p_f2s, f2s),
                      (p_h1s, h1s), (p_h2s, h2s), (p_f3s, f3s),
                      (p_o128, ones128)):
        sh = src.shape
        if len(sh) == 3:
            R.emit("sync", "dma_start",
                   out=dt_t.ap().rearrange("p (l k) -> p l k", l=sh[0]),
                   in_=src.rearrange("l p k -> p l k"), inc=dS, incn=16)
        else:
            R.emit("sync", "dma_start", out=dt_t[:, :], in_=src, inc=dS, incn=16)
    R.emit("sync", "dma_start",
           out=p_cw.ap().rearrange("p (l k) -> p l k", l=NL),
           in_=conv_w.rearrange("l p k -> p l k"), inc=dS, incn=16)
    mal = R.emit("sync", "dma_start",
                 out=p_An.ap().rearrange("p (l k) -> p l k", l=NL),
                 in_=A_log.rearrange("l p k -> p l k"), inc=dS, incn=16)
    R.emit("sync", "dma_start", out=p_sel[:, :], in_=sel, inc=dS, incn=16)
    mA = R.emit("scalar", "activation", out=p_An[:, :], in_=p_An[:, :],
                func=AF.Exp, waits=[mal], inc=sA)
    R.emit("vector", "memset", p_eps[:, :], EPS)
    R.emit("vector", "tensor_single_scalar", out=p_An[:, :], in_=p_An[:, :],
           scalar=-1.0, op=ALU.mult, waits=[mA], inc=sV)
    R.barrier()

    def ln_stats(src_i, nblk, bank_s, bank_b, bank_q, waits, out_i,
                 affine=None, extra_out=None):
        w0 = list(waits)
        mm = None
        for i in range(nblk):
            mm = R.emit("tensor", "matmul", pbk[bank_s][0:1, 0:TC],
                        lhsT=p_o128[:, :], rhs=src_i(i),
                        start=(i == 0), stop=(i == nblk - 1),
                        waits=w0 if i == 0 else ())
        msq = None
        mlast = None
        for i in range(nblk):
            msq = R.emit("scalar", "activation", out=lnsq[:, :], in_=src_i(i),
                         func=AF.Square,
                         waits=(w0 + [mlast]) if i == 0 else [mlast], inc=sA)
            mlast = R.emit("tensor", "matmul", pbk[bank_q][0:1, 0:TC],
                           lhsT=p_o128[:, :], rhs=lnsq[:, :],
                           start=(i == 0), stop=(i == nblk - 1),
                           waits=[msq], inc=sP)
        mst = mlast
        mv = R.emit("vector", "tensor_copy", out=lnrow[:, 0:TC],
                    in_=pbk[bank_s][0:1, 0:TC], waits=[mst] + w0, inc=sV)
        mv = R.emit("vector", "tensor_tensor", out=lnrow[:, TC:2 * TC],
                    in0=lnrow[:, 0:TC], in1=lnrow[:, 0:TC], op=ALU.mult,
                    waits=[mv], inc=sV)
        mv = R.emit("vector", "tensor_tensor", out=lnrow[:, 2 * TC:3 * TC],
                    in0=pbk[bank_q][0:1, 0:TC], in1=lnrow[:, TC:2 * TC],
                    op=ALU.subtract, waits=[mv], inc=sV)
        ma = R.emit("scalar", "activation", out=lnrow[:, 3 * TC:4 * TC],
                    in_=lnrow[:, 2 * TC:3 * TC], func=AF.Ln,
                    bias=p_eps[0:1, 0:1], waits=[mv], inc=sA)
        ma = R.emit("scalar", "activation", out=lnrow_r[:, TC:2 * TC],
                    in_=lnrow[:, 3 * TC:4 * TC], func=AF.Exp, scale=-0.5,
                    waits=[ma], inc=sA)
        mv2 = R.emit("vector", "tensor_copy", out=lnrow_r[:, 0:TC],
                     in_=lnrow[:, 0:TC], waits=[mv], inc=sV)
        mb1 = R.emit("tensor", "matmul", pbk[bank_b][:, 0:TC],
                     lhsT=p_sel[0:1, 0:128], rhs=lnrow_r[:, 0:TC],
                     start=True, stop=True, waits=[mv2] + w0, inc=sP)
        mb2 = R.emit("tensor", "matmul", pbk[bank_b][:, TC:2 * TC],
                     lhsT=p_sel[0:1, 0:128], rhs=lnrow_r[:, TC:2 * TC],
                     start=True, stop=True, waits=[ma], inc=sP)
        last = None
        for i in range(nblk):
            m1 = R.emit("vector", "tensor_tensor", out=lnsq[:, :],
                        in0=src_i(i), in1=pbk[bank_b][:, 0:TC],
                        op=ALU.subtract, waits=[mb1, mb2], inc=sV)
            if affine is None:
                last = R.emit("vector", "tensor_tensor", out=out_i(i),
                              in0=lnsq[:, :], in1=pbk[bank_b][:, TC:2 * TC],
                              op=ALU.mult, waits=[mb2, m1], inc=sV)
                if extra_out is not None:
                    last = R.emit("vector", "tensor_copy", out=extra_out(i),
                                  in_=out_i(i), waits=[last], inc=sV)
            else:
                w_ap, b_ap = affine
                m2 = R.emit("vector", "scalar_tensor_tensor", out=lnsq[:, :],
                            in0=lnsq[:, :], scalar=w_ap(i),
                            in1=pbk[bank_b][:, TC:2 * TC],
                            op0=ALU.mult, op1=ALU.mult, waits=[mb2, m1], inc=sV)
                last = R.emit("vector", "tensor_single_scalar", out=out_i(i),
                              in_=lnsq[:, :], scalar=b_ap(i), op=ALU.add,
                              waits=[m2], inc=sV)
        return last

    for s in range(SPC):
        xs_t = x_fm[s].rearrange("(b p) t -> b p t", p=128)

        R.stage = "P1"
        with ExitStack() as st1:
            p1_x = sb("p1_x", (128, NDB * TC), F32, st1)
            p1_o = sb("p1_o", (128, NDB * TC), F32, st1)
            p1_ob = sb("p1_ob", (128, NDB * TC), BF16, st1)
            prevst = [None, None, None]
            for c in range(NCH):
                cl = slice(c * TC, (c + 1) * TC)
                mld = R.emit("sync", "dma_start",
                             out=p1_x.ap().rearrange("p (b t) -> p b t", b=NDB),
                             in_=xs_t[:, :, cl].rearrange("b p t -> p b t"),
                             waits=[R.now(sV)], inc=dS, incn=16)
                src_i = lambda i: p1_x[:, i * TC:(i + 1) * TC]
                out_i = lambda i: p1_o[:, i * TC:(i + 1) * TC]
                ex_i = lambda i: p1_ob[:, i * TC:(i + 1) * TC]
                last = ln_stats(src_i, NDB, 0, 1, 2,
                                [mld] + [m for m in prevst if m], out_i,
                                affine=None, extra_out=ex_i)
                prevst = [
                    R.emit("sync", "dma_start", out=sp_xn_r[:, :, cl].rearrange("b p t -> p b t"),
                           in_=p1_o.ap().rearrange("p (b t) -> p b t", b=NDB),
                           waits=[last], inc=dS, incn=16),
                    R.emit("sync", "dma_start", out=sp_xn_b[:, :, cl].rearrange("b p t -> p b t"),
                           in_=p1_ob.ap().rearrange("p (b t) -> p b t", b=NDB),
                           waits=[last], inc=dS, incn=16),
                    last]
            R.barrier()
        if s == 0:
            dbg_dump("xn_r", sp_xn_r[:, :, :], (NDB, 128, L), F32)

        for l in range(NL):
            r_src = sp_xn_r if l == 0 else sp_r

            R.stage = "A0"
            with ExitStack() as stA:
                hid = sb("hid", (128, NDB * L), F32R, stA)
                a0x = sb("a0x", (128, NDB * TC), F32, stA)
                prev_last = None
                for c in range(NCH):
                    cl = slice(c * TC, (c + 1) * TC)
                    mld = R.emit("sync", "dma_start",
                                 out=a0x.ap().rearrange("p (b t) -> p b t", b=NDB),
                                 in_=r_src[:, :, cl].rearrange("b p t -> p b t"),
                                 waits=[R.now(sV)], inc=dS, incn=16)
                    src_i = lambda i: a0x[:, i * TC:(i + 1) * TC]
                    out_i = (lambda i, c=c:
                             hid[:, i * L + c * TC: i * L + (c + 1) * TC])
                    prev_last = ln_stats(
                        src_i, NDB, 0, 1, 2,
                        [mld] + ([prev_last] if prev_last else []), out_i,
                        affine=(lambda i, l=l: p_nw[:, l * NDB + i:l * NDB + i + 1],
                                lambda i, l=l: p_nb[:, l * NDB + i:l * NDB + i + 1]))
                R.barrier()

                R.stage = "A1"
                ring_in = sb("ring_in", (128, 4 * NDB * 128), F32R, stA)
                cstg = sb("cstg", (128, 3 * (L + DC - 1)), F32, stA)
                cacc = sb("cacc", (128, L), F32, stA)
                xcst = sb("xcst", (128, 2 * L), F32R, stA)
                zst = sb("zst", (128, 2 * L), BF16, stA)
                NQ = L // 512
                ring_m = [None] * 4
                evm = {}
                msl_prev = None
                for eb in range(2 * NIB):
                    sl_i = eb % 4
                    ring_m[sl_i] = R.emit(
                        "sync", "dma_start",
                        out=ring_in[:, sl_i * 1024:(sl_i + 1) * 1024]
                            .rearrange("p (k e) -> p k e", k=NDB),
                        in_=w_inT[l, :, eb * 128:(eb + 1) * 128]
                            .rearrange("(k p) e -> p k e", p=128),
                        waits=[evm.get(("pe", eb - 4))], inc=dS, incn=16)
                    bank0 = 0 if eb % 2 == 0 else 4
                    mmm = None
                    for k in range(NDB):
                        for q in range(NQ):
                            mmm = R.emit(
                                "tensor", "matmul", pbk[bank0 + q][:, :],
                                lhsT=ring_in[:, sl_i * 1024 + k * 128:
                                             sl_i * 1024 + (k + 1) * 128],
                                rhs=hid[:, k * L + q * 512:k * L + (q + 1) * 512],
                                start=(k == 0), stop=(k == NDB - 1),
                                waits=([ring_m[sl_i], evm.get(("ev", eb - 2))]
                                       if (k == 0 and q == 0) else ()),
                                inc=sP if (k == NDB - 1 and q == NQ - 1) else None)
                    evm[("pe", eb)] = mmm
                    if eb < NIB:
                        db = eb
                        st_i = db % 3
                        stg = cstg[:, st_i * (L + 3):(st_i + 1) * (L + 3)]
                        R.emit("vector", "memset", stg[:, 0:3], 0.0,
                               waits=[mmm])
                        mev = None
                        for q in range(NQ):
                            mev = R.emit("vector", "tensor_copy",
                                         out=stg[:, 3 + q * 512:3 + (q + 1) * 512],
                                         in_=pbk[bank0 + q][:, :],
                                         waits=[mmm], inc=sV)
                        evm[("ev", eb)] = mev
                        cwb = (lambda k, l=l, db=db:
                               p_cw[:, l * NIB * DC + db * DC + k:
                                    l * NIB * DC + db * DC + k + 1])
                        mcv = R.emit("vector", "tensor_scalar", out=cacc[:, :],
                                     in0=stg[:, 0:L], scalar1=cwb(0),
                                     scalar2=None, op0=ALU.mult,
                                     waits=[mev, msl_prev], inc=sV)
                        for k in range(1, DC):
                            mcv = R.emit("vector", "scalar_tensor_tensor",
                                         out=cacc[:, :], in0=stg[:, k:k + L],
                                         scalar=cwb(k), in1=cacc[:, :],
                                         op0=ALU.mult, op1=ALU.add,
                                         waits=[mcv], inc=sV)
                        xst_i = db % 2
                        xstg = xcst[:, xst_i * L:(xst_i + 1) * L]
                        msl = R.emit("scalar", "activation", out=xstg,
                                     in_=cacc[:, :], func=AF.Silu,
                                     bias=p_cb[:, l * NIB + db:l * NIB + db + 1],
                                     waits=[mcv, evm.get(("xcd", db - 2))],
                                     inc=sA)
                        msl_prev = msl
                        evm[("xcd", db)] = R.emit(
                            "sync", "dma_start", out=sp_xc[db], in_=xstg,
                            waits=[msl], inc=dS, incn=16)
                    else:
                        db = eb - NIB
                        zst_i = db % 2
                        zstg = zst[:, zst_i * L:(zst_i + 1) * L]
                        mev = None
                        for q in range(NQ):
                            mev = R.emit("vector", "tensor_copy",
                                         out=zstg[:, q * 512:(q + 1) * 512],
                                         in_=pbk[bank0 + q][:, :],
                                         waits=[mmm, evm.get(("zd", db - 2))],
                                         inc=sV)
                        evm[("ev", eb)] = mev
                        evm[("zd", db)] = R.emit(
                            "sync", "dma_start", out=sp_z[db], in_=zstg,
                            waits=[mev], inc=dS, incn=16)
                R.barrier()
                if s == 0 and l == 0 and dbg:
                    dbg_dump("hid", hid.ap(), (128, NDB * L), F32R, 2)
                    dbg_dump("xc", sp_xc[:, :, :], (NIB, 128, L), F32R, 2)
                    dbg_dump("z", sp_z[:, :, :], (NIB, 128, L), BF16, 2)

            R.stage = "A3"
            with ExitStack() as st3:
                w_xp_sb = sb("w_xp_sb", (128, NIB * 96), F32R, st3)
                w_dtp_sb = sb("w_dtp_sb", (DR, DI), F32R, st3)
                xcc = sb("xcc", (128, NIB * TC), F32R, st3)
                dtr_sb = sb("dtr_sb", (DR, 2 * TC), F32R, st3)
                dtst = sb("dtst", (128, NIB * TC), F32, st3)
                mxp = R.emit("sync", "dma_start",
                             out=w_xp_sb.ap().rearrange("p (k c) -> p k c", k=NIB),
                             in_=w_xpT[l].rearrange("(k p) c -> p k c", p=128),
                             inc=dS, incn=16)
                mdtp = R.emit("sync", "dma_start", out=w_dtp_sb[:, :],
                              in_=w_dtpT[l], inc=dS, incn=16)
                prev_mC = None
                prev_dma = None
                dtev = {}
                for c in range(NCH):
                    cl = slice(c * TC, (c + 1) * TC)
                    mxc = R.emit("sync", "dma_start",
                                 out=xcc.ap().rearrange("p (k t) -> p k t", k=NIB),
                                 in_=sp_xc[:, :, cl].rearrange("k p t -> p k t"),
                                 waits=[prev_mC], inc=dS, incn=16)
                    m_dtr = m_B = m_C = None
                    for k in range(NIB):
                        m_dtr = R.emit("tensor", "matmul", pbk[6][0:64, 0:TC],
                                       lhsT=w_xp_sb[:, k * 96:k * 96 + 64],
                                       rhs=xcc[:, k * TC:(k + 1) * TC],
                                       start=(k == 0), stop=(k == NIB - 1),
                                       waits=([mxc, mxp, R.now(sV)]
                                              if k == 0 else ()),
                                       inc=sP if k == NIB - 1 else None)
                    for k in range(NIB):
                        m_B = R.emit("tensor", "matmul", pbk[7][0:16, 0:TC],
                                     lhsT=w_xp_sb[:, k * 96 + 64:k * 96 + 80],
                                     rhs=xcc[:, k * TC:(k + 1) * TC],
                                     start=(k == 0), stop=(k == NIB - 1),
                                     inc=sP if k == NIB - 1 else None)
                    for k in range(NIB):
                        m_C = R.emit("tensor", "matmul", pbk[4][0:16, 0:TC],
                                     lhsT=w_xp_sb[:, k * 96 + 80:k * 96 + 96],
                                     rhs=xcc[:, k * TC:(k + 1) * TC],
                                     start=(k == 0), stop=(k == NIB - 1),
                                     inc=sP if k == NIB - 1 else None)
                    prev_mC = m_C
                    mBe = R.emit("vector", "tensor_copy", out=Bc[:, cl],
                                 in_=pbk[7][0:16, 0:TC], waits=[m_B], inc=sV)
                    mCe = R.emit("vector", "tensor_copy", out=Cc[:, cl],
                                 in_=pbk[4][0:16, 0:TC], waits=[m_C], inc=sV)
                    dt_i = c % 2
                    mDe = R.emit("vector", "tensor_copy",
                                 out=dtr_sb[:, dt_i * TC:(dt_i + 1) * TC],
                                 in_=pbk[6][0:64, 0:TC], waits=[m_dtr], inc=sV)
                    wA3 = R.now(sA)
                    for odb in range(NIB):
                        bk = 2 + odb % 2
                        mdt = R.emit("tensor", "matmul", pbk[bk][:, 0:TC],
                                     lhsT=w_dtp_sb[:, odb * 128:(odb + 1) * 128],
                                     rhs=dtr_sb[:, dt_i * TC:(dt_i + 1) * TC],
                                     start=True, stop=True,
                                     waits=[mDe, mdtp,
                                            dtev.get(odb - 2, wA3)],
                                     inc=sP)
                        dtev[odb] = R.emit(
                            "scalar", "activation",
                            out=dtst[:, odb * TC:(odb + 1) * TC],
                            in_=pbk[bk][:, 0:TC], func=AF.Identity,
                            bias=p_dtb[:, l * NIB + odb:l * NIB + odb + 1],
                            waits=[mdt, prev_dma], inc=sA)
                    prev_dma = R.emit(
                        "sync", "dma_start", out=sp_dtr[:, :, cl].rearrange("k p t -> p k t"),
                        in_=dtst.ap().rearrange("p (k t) -> p k t", k=NIB),
                        waits=[dtev[o] for o in range(NIB)], inc=dS, incn=16)
                    dtev = {}
                R.barrier()
                if s == 0 and l == 0 and dbg:
                    dbg_dump("dtr", sp_dtr[:, :, :], (NIB, 128, L), F32, 2)
                    dbg_dump("Bc", Bc[:, :], (DS, L), F32R, 2)
                    dbg_dump("Cc", Cc[:, :], (DS, L), F32R, 2)

            R.stage = "B"
            with ExitStack() as stB:
                bs_dt = sb("bs_dt", (128, GD * TH), F32, stB)
                bs_du = sb("bs_du", (128, GD * TH), F32, stB)
                bs_y = sb("bs_y", (128, GD * TH), F32, stB)
                bs_xc = sb("bs_xc", (128, 2 * TH), F32R, stB)
                slot = sb("slot", (128, 4 * TH), F32, stB)
                SL = [slot[:, i * TH:(i + 1) * TH] for i in range(4)]
                XS = [bs_xc[:, i * TH:(i + 1) * TH] for i in range(2)]
                prev_b = [None, None]
                last_aexp = None
                last_spill = None
                for g in range(NG):
                    for h in range(2):
                        hl = slice(h * TH, (h + 1) * TH)
                        mdl = R.emit("sync", "dma_start",
                                     out=bs_du.ap()
                                         .rearrange("p (j t) -> p j t", j=GD),
                                     in_=sp_dtr[g * GD:(g + 1) * GD, :, hl].rearrange("j p t -> p j t"),
                                     waits=[R.now(sV)], inc=dS, incn=16)
                        m_addp = None
                        for j in range(GD):
                            raw = bs_du[:, j * TH:(j + 1) * TH]
                            m1 = R.emit("scalar", "activation", out=SL[0],
                                        in_=raw, func=AF.Abs,
                                        waits=([mdl, R.now(sV)] if j == 0
                                               else [mdl, m_addp]), inc=sA)
                            m1 = R.emit("scalar", "activation", out=SL[1],
                                        in_=SL[0], func=AF.Exp, scale=-1.0,
                                        waits=[m1], inc=sA)
                            m1 = R.emit("scalar", "activation", out=SL[0],
                                        in_=SL[1], func=AF.Ln, bias=1.0,
                                        waits=[m1], inc=sA)
                            m2 = R.emit("vector", "tensor_single_scalar",
                                        out=SL[2], in_=raw, scalar=0.0,
                                        op=ALU.max, waits=[mdl], inc=sV)
                            m_addp = R.emit("vector", "tensor_tensor",
                                            out=bs_dt[:, j * TH:(j + 1) * TH],
                                            in0=SL[0], in1=SL[2], op=ALU.add,
                                            waits=[m1, m2], inc=sV)
                        mdt_done = R.now(sV)
                        mxl = [None, None]
                        mdu = None
                        for j in range(GD):
                            pp = j % 2
                            mxl[pp] = R.emit(
                                "sync", "dma_start", out=XS[pp],
                                in_=sp_xc[g * GD + j, :, hl],
                                waits=[mdt_done, mdu], inc=dS, incn=16)
                            mdu = R.emit("vector", "tensor_tensor",
                                         out=bs_du[:, j * TH:(j + 1) * TH],
                                         in0=bs_dt[:, j * TH:(j + 1) * TH],
                                         in1=XS[pp], op=ALU.mult,
                                         waits=[mxl[pp]], inc=sV)
                        mdu_done = R.now(sV)
                        prev_my = None
                        prev_msc = None
                        for n in range(DS):
                            mB2 = None
                            mC2 = None
                            R.emit("tensor", "matmul", pbk[4][:, :],
                                   lhsT=p_sel[:, n * 128:(n + 1) * 128],
                                   rhs=Bc[:, h * TH:h * TH + 512],
                                   start=True, stop=True,
                                   waits=[prev_b[0], mdu_done])
                            mB2 = R.emit("tensor", "matmul", pbk[5][:, :],
                                         lhsT=p_sel[:, n * 128:(n + 1) * 128],
                                         rhs=Bc[:, h * TH + 512:(h + 1) * TH],
                                         start=True, stop=True, inc=sP)
                            R.emit("tensor", "matmul", pbk[6][:, :],
                                   lhsT=p_sel[:, n * 128:(n + 1) * 128],
                                   rhs=Cc[:, h * TH:h * TH + 512],
                                   start=True, stop=True, waits=[prev_b[1]])
                            mC2 = R.emit("tensor", "matmul", pbk[7][:, :],
                                         lhsT=p_sel[:, n * 128:(n + 1) * 128],
                                         rhs=Cc[:, h * TH + 512:(h + 1) * TH],
                                         start=True, stop=True, inc=sP)
                            for j in range(GD):
                                db = g * GD + j
                                ma = R.emit(
                                    "scalar", "activation", out=SL[0],
                                    in_=bs_dt[:, j * TH:(j + 1) * TH],
                                    func=AF.Exp,
                                    scale=p_An[:, l * NIB * DS + db * DS + n:
                                               l * NIB * DS + db * DS + n + 1],
                                    waits=[prev_msc if prev_msc else mdu_done],
                                    inc=sA)
                                last_aexp = ma
                                mb1 = R.emit("vector", "tensor_tensor",
                                             out=SL[1][:, 0:512],
                                             in0=bs_du[:, j * TH:j * TH + 512],
                                             in1=pbk[4][:, :], op=ALU.mult,
                                             waits=[mB2], inc=sV)
                                mb2 = R.emit("vector", "tensor_tensor",
                                             out=SL[1][:, 512:1024],
                                             in0=bs_du[:,
                                                       j * TH + 512:(j + 1) * TH],
                                             in1=pbk[5][:, :], op=ALU.mult,
                                             inc=sV)
                                init = (0.0 if h == 0 else
                                        carry[:, j * DS + n:j * DS + n + 1])
                                msc = R.emit("vector", "tensor_tensor_scan",
                                             out=SL[2], data0=SL[0],
                                             data1=SL[1], initial=init,
                                             op0=ALU.mult, op1=ALU.add,
                                             waits=[ma], inc=sV)
                                prev_msc = msc
                                if h == 0:
                                    R.emit("vector", "tensor_copy",
                                           out=carry[:, j * DS + n:
                                                     j * DS + n + 1],
                                           in_=SL[2][:, TH - 1:TH], inc=sV)
                                mp1 = R.emit("vector", "tensor_tensor",
                                             out=SL[3][:, 0:512],
                                             in0=SL[2][:, 0:512],
                                             in1=pbk[6][:, :], op=ALU.mult,
                                             waits=[mC2, prev_my], inc=sV)
                                mp2 = R.emit("vector", "tensor_tensor",
                                             out=SL[3][:, 512:1024],
                                             in0=SL[2][:, 512:1024],
                                             in1=pbk[7][:, :], op=ALU.mult,
                                             inc=sV)
                                prev_b = [mb2, mp2]
                                if n == 0:
                                    my = R.emit("gpsimd", "tensor_copy",
                                                out=bs_y[:, j * TH:(j + 1) * TH],
                                                in_=SL[3],
                                                waits=[mp1, mp2, last_spill],
                                                inc=sO)
                                else:
                                    my = R.emit("gpsimd", "tensor_tensor",
                                                out=bs_y[:, j * TH:(j + 1) * TH],
                                                in0=bs_y[:, j * TH:(j + 1) * TH],
                                                in1=SL[3], op=ALU.add,
                                                waits=[mp1, mp2], inc=sO)
                                prev_my = my
                        last_spill = R.emit(
                            "sync", "dma_start",
                            out=sp_y[g * GD:(g + 1) * GD, :, hl].rearrange("j p t -> p j t"),
                            in_=bs_y.ap().rearrange("p (j t) -> p j t", j=GD),
                            waits=[prev_my], inc=dS, incn=16)
                R.barrier()
                if s == 0 and l == 0 and dbg:
                    dbg_dump("y", sp_y[:, :, :], (NIB, 128, L), F32, 2)

            R.stage = "C"
            with ExitStack() as stC:
                c_y = sb("c_y", (128, NIB * TC), F32, stC)
                c_xc = sb("c_xc", (128, NIB * TC), F32R, stC)
                c_z = sb("c_z", (128, NIB * TC), BF16, stC)
                c_zs = sb("c_zs", (128, TC), F32, stC)
                c_yg = sb("c_yg", (128, NIB * TC), F32R, stC)
                c_r = sb("c_r", (128, NDB * TC), F32, stC)
                ring_op = sb("ring_op", (128, 2 * NIB * 128), F32R, stC)
                rop_m = [None, None]
                opev = {}
                for c in range(NCH):
                    cl = slice(c * TC, (c + 1) * TC)
                    wpre = [R.now(sV), R.now(sA), R.now(sP)]
                    mly = R.emit("sync", "dma_start",
                                 out=c_y.ap().rearrange("p (k t) -> p k t", k=NIB),
                                 in_=sp_y[:, :, cl].rearrange("k p t -> p k t"), waits=wpre, inc=dS, incn=16)
                    mlx = R.emit("sync", "dma_start",
                                 out=c_xc.ap().rearrange("p (k t) -> p k t", k=NIB),
                                 in_=sp_xc[:, :, cl].rearrange("k p t -> p k t"), inc=dS, incn=16)
                    mlz = R.emit("sync", "dma_start",
                                 out=c_z.ap().rearrange("p (k t) -> p k t", k=NIB),
                                 in_=sp_z[:, :, cl].rearrange("k p t -> p k t"), inc=dS, incn=16)
                    mlr = R.emit("sync", "dma_start",
                                 out=c_r.ap().rearrange("p (k t) -> p k t", k=NDB),
                                 in_=r_src[:, :, cl].rearrange("b p t -> p b t"),
                                 waits=[R.now(sV)], inc=dS, incn=16)
                    mgate = None
                    for db in range(NIB):
                        ts_ = slice(db * TC, (db + 1) * TC)
                        m1 = R.emit("vector", "scalar_tensor_tensor",
                                    out=c_y[:, ts_], in0=c_xc[:, ts_],
                                    scalar=p_D[:, l * NIB + db:l * NIB + db + 1],
                                    in1=c_y[:, ts_], op0=ALU.mult, op1=ALU.add,
                                    waits=[mly, mlx], inc=sV)
                        m2 = R.emit("scalar", "activation", out=c_zs[:, :],
                                    in_=c_z[:, ts_], func=AF.Silu,
                                    waits=[mlz, mgate], inc=sA)
                        mgate = R.emit("vector", "tensor_tensor",
                                       out=c_yg[:, ts_], in0=c_y[:, ts_],
                                       in1=c_zs[:, :], op=ALU.mult,
                                       waits=[m2, m1], inc=sV)
                    for odb in range(NDB):
                        sl_i = odb % 2
                        rop_m[sl_i] = R.emit(
                            "sync", "dma_start",
                            out=ring_op[:, sl_i * 2048:(sl_i + 1) * 2048]
                                .rearrange("p (k e) -> p k e", k=NIB),
                            in_=w_outT[l, :, odb * 128:(odb + 1) * 128]
                                .rearrange("(k p) e -> p k e", p=128),
                            waits=[opev.get((c, odb - 2))], inc=dS, incn=16)
                        bk = odb % 2
                        mmo = None
                        for k in range(NIB):
                            mmo = R.emit(
                                "tensor", "matmul", pbk[bk][:, 0:TC],
                                lhsT=ring_op[:, sl_i * 2048 + k * 128:
                                             sl_i * 2048 + (k + 1) * 128],
                                rhs=c_yg[:, k * TC:(k + 1) * TC],
                                start=(k == 0), stop=(k == NIB - 1),
                                waits=([mgate, rop_m[sl_i],
                                        opev.get((c, odb - 2)),
                                        opev.get((c - 1, NDB - 2 + odb % 2))]
                                       if k == 0 else ()),
                                inc=sP if k == NIB - 1 else None)
                        opev[(c, odb)] = R.emit(
                            "vector", "tensor_tensor",
                            out=c_r[:, odb * TC:(odb + 1) * TC],
                            in0=c_r[:, odb * TC:(odb + 1) * TC],
                            in1=pbk[bk][:, 0:TC], op=ALU.add,
                            waits=[mmo, mlr], inc=sV)
                    R.emit("sync", "dma_start", out=sp_r[:, :, cl].rearrange("b p t -> p b t"),
                           in_=c_r.ap().rearrange("p (k t) -> p k t", k=NDB),
                           waits=[opev[(c, o)] for o in range(NDB)],
                           inc=dS, incn=16)
                R.barrier()
                if s == 0 and l == 0 and dbg:
                    dbg_dump("r1", sp_r[:, :, :], (NDB, 128, L), F32, 2)

        R.stage = "FFN"
        with ExitStack() as stF:
            fx = sb("fx", (128, NDB * TF), BF16, stF)
            fhg = sb("fhg", (128, NHB * TF), BF16, stF)
            fhg2 = sb("fhg2", (128, NHB * TF), BF16, stF)
            fw12 = sb("fw12", (128, 4 * NDB * 128), BF16, stF)
            fwh = sb("fwh", (128, 2 * NHB * 128), BF16, stF)
            ftmp = sb("ftmp", (128, TF), F32, stF)
            fst = sb("fst", (128, 2 * TF), BF16, stF)
            for ft in range(NFT):
                fl = slice(ft * TF, (ft + 1) * TF)
                mfx = R.emit("gpsimd", "dma_start",
                             out=fx.ap().rearrange("p (k t) -> p k t", k=NDB),
                             in_=sp_xn_b[:, :, fl].rearrange("b p t -> p b t"),
                             waits=[R.now(sP)], inc=dG, incn=16)
                ev = {}
                fr_m = [None] * 4
                for hb in range(NHB):
                    for w_i, wsrc in enumerate((w_f1T, w_f2T)):
                        it = 2 * hb + w_i
                        sl_i = it % 4
                        fr_m[sl_i] = R.emit(
                            "gpsimd", "dma_start",
                            out=fw12[:, sl_i * 1024:(sl_i + 1) * 1024]
                                .rearrange("p (k e) -> p k e", k=NDB),
                            in_=wsrc[:, hb * 128:(hb + 1) * 128]
                                .rearrange("(k p) e -> p k e", p=128),
                            waits=[ev.get(("c", it - 4))], inc=dG, incn=16)
                        bk = it % 2
                        mmf = None
                        for k in range(NDB):
                            mmf = R.emit(
                                "tensor", "matmul", pbk[bk][:, :],
                                lhsT=fw12[:, sl_i * 1024 + k * 128:
                                          sl_i * 1024 + (k + 1) * 128],
                                rhs=fx[:, k * TF:(k + 1) * TF],
                                start=(k == 0), stop=(k == NDB - 1),
                                waits=([mfx, fr_m[sl_i], ev.get(("c", it - 2))]
                                       if k == 0 else ()),
                                inc=sP if k == NDB - 1 else None)
                        if w_i == 0:
                            m1 = R.emit("scalar", "activation", out=ftmp[:, :],
                                        in_=pbk[bk][:, :], func=AF.Silu,
                                        scale=p_f1s[:, hb:hb + 1],
                                        waits=[mmf, ev.get(("c", it - 1))],
                                        inc=sA)
                            ev[("c", it)] = m1
                            ev[("silu", hb)] = m1
                        else:
                            m2 = R.emit("vector", "scalar_tensor_tensor",
                                        out=fhg[:, hb * TF:(hb + 1) * TF],
                                        in0=pbk[bk][:, :],
                                        scalar=p_f2s[:, hb:hb + 1],
                                        in1=ftmp[:, :], op0=ALU.mult,
                                        op1=ALU.mult,
                                        waits=[mmf, ev[("silu", hb)]], inc=sV)
                            ev[("c", it)] = m2
                mhg = R.now(sV)
                for hop, (wsrc, sclt, src_buf) in enumerate(
                        ((w_h1T, p_h1s, fhg), (w_h2T, p_h2s, fhg2))):
                    hr_m = [None, None]
                    for hb in range(NHB):
                        sl_i = hb % 2
                        hr_m[sl_i] = R.emit(
                            "gpsimd", "dma_start",
                            out=fwh[:, sl_i * 4096:(sl_i + 1) * 4096]
                                .rearrange("p (k e) -> p k e", k=NHB),
                            in_=wsrc[:, hb * 128:(hb + 1) * 128]
                                .rearrange("(k p) e -> p k e", p=128),
                            waits=[ev.get(("h", hop, hb - 2), R.now(sP))],
                            inc=dG, incn=16)
                        bk = hb % 2
                        mmf = None
                        for k in range(NHB):
                            mmf = R.emit(
                                "tensor", "matmul", pbk[bk][:, :],
                                lhsT=fwh[:, sl_i * 4096 + k * 128:
                                         sl_i * 4096 + (k + 1) * 128],
                                rhs=src_buf[:, k * TF:(k + 1) * TF],
                                start=(k == 0), stop=(k == NHB - 1),
                                waits=([mhg, hr_m[sl_i],
                                        ev.get(("h", hop, hb - 2))]
                                       if k == 0 else ()),
                                inc=sP if k == NHB - 1 else None)
                        if hop == 0:
                            mg = R.emit("scalar", "activation",
                                        out=fhg2[:, hb * TF:(hb + 1) * TF],
                                        in_=pbk[bk][:, :], func=AF.Gelu,
                                        scale=p_h1s[:, hb:hb + 1],
                                        waits=[mmf], inc=sA)
                            ev[("h", hop, hb)] = mg
                        else:
                            st_i = hb % 2
                            mg = R.emit("scalar", "activation",
                                        out=fst[:, st_i * TF:(st_i + 1) * TF],
                                        in_=pbk[bk][:, :], func=AF.Gelu,
                                        scale=p_h2s[:, hb:hb + 1],
                                        waits=[mmf, ev.get(("hst", hb - 2))],
                                        inc=sA)
                            ev[("h", hop, hb)] = mg
                            ev[("hst", hb)] = R.emit(
                                "gpsimd", "dma_start", out=sp_hg3[hb, :, fl],
                                in_=fst[:, st_i * TF:(st_i + 1) * TF],
                                waits=[mg], inc=dG, incn=16)
                    if hop == 0:
                        mhg = R.now(sA)
            R.barrier()
        if s == 0 and dbg:
            dbg_dump("hg3", sp_hg3[:, :, :], (NHB, 128, L), BF16, 2)
            dbg_dump("rfin", sp_r[:, :, :], (NDB, 128, L), F32, 2)

        R.stage = "FIN"
        with ExitStack() as stG:
            g_hg = sb("g_hg", (128, NHB * TF), BF16, stG)
            g_mo = sb("g_mo", (128, NDB * TF), F32, stG)
            g_x = sb("g_x", (128, NDB * TF), F32, stG)
            g_o = sb("g_o", (128, NDB * TF), F32, stG)
            ring_f3 = sb("ring_f3", (128, 2 * NHB * 128), BF16, stG)
            f3ev = {}
            fr3 = [None, None]
            for ft in range(NFT):
                fl = slice(ft * TF, (ft + 1) * TF)
                wpre = [R.now(sP), R.now(sV)]
                mhl = R.emit("gpsimd", "dma_start",
                             out=g_hg.ap().rearrange("p (k t) -> p k t", k=NHB),
                             in_=sp_hg3[:, :, fl].rearrange("k p t -> p k t"), waits=wpre, inc=dG, incn=16)
                mrl = R.emit("sync", "dma_start",
                             out=g_x.ap().rearrange("p (k t) -> p k t", k=NDB),
                             in_=sp_r[:, :, fl].rearrange("b p t -> p b t"), waits=wpre, inc=dS, incn=16)
                lastln = []
                for half in range(TF // TC):
                    off = half * TC
                    src_i = (lambda i, off=off:
                             g_x[:, i * TF + off:i * TF + off + TC])
                    out_i = (lambda i, off=off:
                             g_mo[:, i * TF + off:i * TF + off + TC])
                    la = ln_stats(src_i, NDB, 2, 3, 4,
                                  [mrl, R.now(sV)] + lastln, out_i,
                                  affine=(lambda i: p_nfw[:, i:i + 1],
                                          lambda i: p_nfb[:, i:i + 1]))
                    lastln.append(la)
                mxl = R.emit("sync", "dma_start",
                             out=g_x.ap().rearrange("p (k t) -> p k t", k=NDB),
                             in_=xs_t[:, :, fl].rearrange("b p t -> p b t"), waits=lastln, inc=dS, incn=16)
                for odb in range(NDB):
                    sl_i = odb % 2
                    fr3[sl_i] = R.emit(
                        "gpsimd", "dma_start",
                        out=ring_f3[:, sl_i * 4096:(sl_i + 1) * 4096]
                            .rearrange("p (k e) -> p k e", k=NHB),
                        in_=w_f3T[:, odb * 128:(odb + 1) * 128]
                            .rearrange("(k p) e -> p k e", p=128),
                        waits=[f3ev.get((ft, odb - 2))], inc=dG, incn=16)
                    bk = odb % 2
                    mmf = None
                    for k in range(NHB):
                        mmf = R.emit(
                            "tensor", "matmul", pbk[bk][:, :],
                            lhsT=ring_f3[:, sl_i * 4096 + k * 128:
                                         sl_i * 4096 + (k + 1) * 128],
                            rhs=g_hg[:, k * TF:(k + 1) * TF],
                            start=(k == 0), stop=(k == NHB - 1),
                            waits=([mhl, fr3[sl_i], f3ev.get((ft, odb - 2)),
                                    f3ev.get((ft - 1, NDB - 2 + odb % 2))]
                                   if k == 0 else ()),
                            inc=sP if k == NHB - 1 else None)
                    m1 = R.emit("vector", "scalar_tensor_tensor",
                                out=g_o[:, odb * TF:(odb + 1) * TF],
                                in0=pbk[bk][:, :], scalar=p_f3s[:, odb:odb + 1],
                                in1=g_mo[:, odb * TF:(odb + 1) * TF],
                                op0=ALU.mult, op1=ALU.add,
                                waits=[mmf] + lastln, inc=sV)
                    f3ev[(ft, odb)] = R.emit(
                        "vector", "tensor_tensor",
                        out=g_o[:, odb * TF:(odb + 1) * TF],
                        in0=g_o[:, odb * TF:(odb + 1) * TF],
                        in1=g_x[:, odb * TF:(odb + 1) * TF], op=ALU.add,
                        waits=[mxl, m1], inc=sV)
                R.emit("sync", "dma_start",
                       out=out_fm[s].rearrange("(k p) t -> k p t", p=128)
                           [:, :, fl].rearrange("k p t -> p k t"),
                       in_=g_o.ap().rearrange("p (k t) -> p k t", k=NDB),
                       waits=[f3ev[(ft, o)] for o in range(NDB)],
                       inc=dS, incn=16)
            R.barrier()

    R.finish()

    with nc.Block() as blk:
        blk.tensor(lambda t: [f(t) for f in R.sched["tensor"]])
        blk.vector(lambda v: [f(v) for f in R.sched["vector"]])
        blk.scalar(lambda sc: [f(sc) for f in R.sched["scalar"]])
        blk.gpsimd(lambda g: [f(g) for f in R.sched["gpsimd"]])
        blk.sync(lambda sy: [f(sy) for f in R.sched["sync"]])

    es.close()
    return nc


def kernel(**inputs):
    host = _prep_host(inputs)
    nc = build_program()
    in_maps = []
    for c in range(NCORES):
        m = dict(host)
        m["x_fm"] = np.ascontiguousarray(host["x_fm"][c * SPC:(c + 1) * SPC])
        in_maps.append(m)
    res = run_bass_kernel_spmd(nc, in_maps, core_ids=list(range(NCORES)))
    full = np.concatenate([r["out_fm"] for r in res.results], axis=0)
    return np.ascontiguousarray(full.transpose(0, 2, 1)).astype(np.float32)


# version 02
# speedup vs baseline: 1.0224x; 1.0224x over previous
from contextlib import ExitStack

import numpy as np
import concourse.bass as bass
import concourse.mybir as mybir
from concourse.bass_utils import run_bass_kernel_spmd

F32 = mybir.dt.float32
F32R = mybir.dt.float32r
BF16 = mybir.dt.bfloat16
AF = mybir.ActivationFunctionType
ALU = mybir.AluOpType

B, L, DM = 16, 2048, 1024
DI, DS, DR, DC, NL = 2048, 16, 64, 4, 4
DH = 4096
NCORES, SPC = 8, 2
NDB = DM // 128
NIB = DI // 128
NHB = DH // 128
TC = 256
NCH = L // TC
TH = 1024
TF = 512
NFT = L // TF
EPS = 1e-5
GD = 4
NG = NIB // GD

ENGS = ("tensor", "vector", "scalar", "gpsimd", "sync")
AUDIT = {}


class SemT:
    def __init__(self, h, name):
        self.h, self.n, self.name = h, 0, name


def _is_ap(x):
    return hasattr(x, "tensor") and hasattr(x, "offset") and hasattr(x, "ap")


def _ap_access(x, is_w, sbuf_addr):
    t = x.tensor
    sp = str(t.space)
    pairs = [list(p) for p in x.ap]
    esz = {"dt.float32": 4, "dt.float32r": 4, "dt.bfloat16": 2}.get(
        str(x.dtype), 4)
    if "SBUF" in sp:
        rowel = int(t.shape[1]) if len(t.shape) > 1 else 1
        off = int(x.offset)
        col = off % rowel
        span = 1 + sum((c - 1) * abs(s) for s, c in pairs[1:])
        base = sbuf_addr.get(t.name, (0,))[0]
        return ("SBUF", "all", base + col * esz, base + (col + span) * esz,
                is_w)
    if "PSUM" in sp:
        return ("PSUM", t.name, 0, 1, is_w)
    off = int(x.offset)
    span = 1 + sum((c - 1) * abs(s) for s, c in pairs)
    return ("DRAM", t.name, off * esz, (off + span) * esz, is_w)


class Rec:
    def __init__(self, sems, serial=False):
        self.sched = {e: [] for e in ENGS}
        self.lastw = {e: {} for e in ENGS}
        self.sems = sems
        self.nbar = 0
        self.serial = serial
        self.stage = "pro"
        self.audit = None
        self.sbuf_addr = {}
        self.ntag = 0

    def _waits(self, eng, waits):
        out = []
        for m in waits:
            if m is None:
                continue
            sem, val = m
            if self.lastw[eng].get(sem, -1) >= val:
                continue
            self.lastw[eng][sem] = val
            out.append((sem, val))
        return out

    def emit(self, _eng, _opname, *args, waits=(), inc=None, incn=1, **kw):
        eng = _eng
        if self.serial and (self.serial is True or eng in self.serial
                            or f"{eng}@{self.stage}" in self.serial):
            waits = list(waits) + [(s, s.n) for s in self.sems.values()
                                   if s.name != "bar"]
        wl = self._waits(eng, waits)
        marker = None
        if inc is not None:
            if inc.name in ("dS", "dG"):
                key = inc.name
                pool = self.pools[key]
                inc = pool[self.rr[key] % len(pool)]
                self.rr[key] += 1
            inc.n += incn
            marker = (inc, inc.n)
        if self.audit is not None:
            self.ntag += 1
            accesses = []
            if _opname in ("memset", "matmul"):
                accesses.append(_ap_access(args[0], True, self.sbuf_addr))
            for k, v in kw.items():
                if _is_ap(v):
                    accesses.append(_ap_access(v, k.startswith("out"),
                                               self.sbuf_addr))
            self.audit.append({
                "kind": "dma" if _opname == "dma_start" else "ins",
                "eng": eng,
                "waits": [(s.name, v) for s, v in wl],
                "inc": (inc.name, inc.n, incn) if inc is not None else None,
                "accesses": accesses,
                "tag": f"{self.stage}#{_opname}#{self.ntag}",
            })

        def f(e, wl=wl, opname=_opname, args=args, kw=kw, inc=inc, incn=incn):
            for s, v in wl:
                e.wait_ge(s.h, v)
            ins = getattr(e, opname)(*args, **kw)
            if inc is not None:
                ins.then_inc(inc.h, incn)

        self.sched[eng].append(f)
        return marker

    def now(self, sem):
        return (sem, sem.n)

    def barrier(self):
        bar = self.sems["bar"]
        self.nbar += 1
        tgt = 5 * self.nbar
        for eng in ENGS:
            if eng == "sync":
                dw = self._waits(eng, [(s, s.n) for s in self.pools["dS"]])
            elif eng == "gpsimd":
                dw = self._waits(eng, [(s, s.n) for s in self.pools["dG"]])
            else:
                dw = []

            def f(e, dw=dw, bar=bar, tgt=tgt):
                for s, v in dw:
                    e.wait_ge(s.h, v)
                e.sem_inc(bar.h, 1)
                e.wait_ge(bar.h, tgt)

            self.sched[eng].append(f)
            self.lastw[eng][bar] = tgt
        bar.n = tgt
        if self.audit is not None:
            self.audit.append({"kind": "barrier"})

    def finish(self):
        ds_list = [(s, s.n) for s in self.pools["dS"]]
        dg_list = [(s, s.n) for s in self.pools["dG"]]

        def fs(e, lst=ds_list):
            for s, v in lst:
                e.wait_ge(s.h, v)

        def fg(e, lst=dg_list):
            for s, v in lst:
                e.wait_ge(s.h, v)

        self.sched["sync"].append(fs)
        self.sched["gpsimd"].append(fg)


def _prep_host(inp):
    d = {}
    x = np.asarray(inp["L0_attn_input_ids"], dtype=np.float32)
    d["x_fm"] = np.ascontiguousarray(x.transpose(0, 2, 1))

    d["w_inT"] = np.ascontiguousarray(
        np.asarray(inp["L0_attn_in_proj_w"], np.float32).transpose(0, 2, 1))
    d["w_xpT"] = np.ascontiguousarray(
        np.asarray(inp["L0_attn_x_proj_w"], np.float32).transpose(0, 2, 1))
    d["w_dtpT"] = np.ascontiguousarray(
        np.asarray(inp["L0_attn_dt_proj_w"], np.float32).transpose(0, 2, 1))
    d["w_outT"] = np.ascontiguousarray(
        np.asarray(inp["L0_attn_out_proj_w"], np.float32).transpose(0, 2, 1))
    d["w_f1T"] = np.ascontiguousarray(np.asarray(inp["L0_ffn_w1_w"], np.float32).T)
    d["w_f2T"] = np.ascontiguousarray(np.asarray(inp["L0_ffn_w2_w"], np.float32).T)
    d["w_h1T"] = np.ascontiguousarray(np.asarray(inp["L0_ffn_h1_w"], np.float32).T)
    d["w_h2T"] = np.ascontiguousarray(np.asarray(inp["L0_ffn_h2_w"], np.float32).T)
    d["w_f3T"] = np.ascontiguousarray(np.asarray(inp["L0_ffn_w3_w"], np.float32).T)

    def p128(a, last):
        a = np.asarray(a, np.float32)
        return np.ascontiguousarray(a.reshape(a.shape[0], last, 128).transpose(0, 2, 1))

    d["norm_w"] = p128(inp["L0_attn_norm_w"], NDB)
    d["norm_b"] = p128(inp["L0_attn_norm_b"], NDB)
    d["conv_b"] = p128(inp["L0_attn_conv_b"], NIB)
    d["dtp_b"] = p128(inp["L0_attn_dt_proj_b"], NIB)
    d["Dp"] = p128(inp["L0_attn_D"], NIB)
    cw = np.asarray(inp["L0_attn_conv_w"], np.float32)
    d["conv_w"] = np.ascontiguousarray(
        cw.reshape(NL, NIB, 128, DC).transpose(0, 2, 1, 3).reshape(NL, 128, NIB * DC))
    al = np.asarray(inp["L0_attn_A_log"], np.float32)
    d["A_log"] = np.ascontiguousarray(
        al.reshape(NL, NIB, 128, DS).transpose(0, 2, 1, 3).reshape(NL, 128, NIB * DS))
    d["normf_w"] = np.ascontiguousarray(
        np.asarray(inp["L0_attn_normf_w"], np.float32).reshape(NDB, 128).T)
    d["normf_b"] = np.ascontiguousarray(
        np.asarray(inp["L0_attn_normf_b"], np.float32).reshape(NDB, 128).T)

    def s128(a):
        return np.ascontiguousarray(np.asarray(a, np.float32).reshape(-1, 128).T)

    d["f1s"] = s128(inp["L0_ffn_w1_s"])
    d["f2s"] = s128(inp["L0_ffn_w2_s"])
    d["h1s"] = s128(inp["L0_ffn_h1_s"])
    d["h2s"] = s128(inp["L0_ffn_h2_s"])
    d["f3s"] = s128(inp["L0_ffn_w3_s"])

    d["sel"] = np.ascontiguousarray(
        np.repeat(np.eye(DS, dtype=np.float32)[:, :, None], 128, axis=2)
        .transpose(1, 0, 2).reshape(DS, DS * 128))
    d["ones128"] = np.full((128, 1), 1.0 / DM, np.float32)
    return d


def build_program(dbg=False, serial=False, audit=False):
    nc = bass.Bass("TRN2", target_bir_lowering=False)

    def din(name, shape, dt):
        return nc.dram_tensor(name, list(shape), dt, kind="ExternalInput").ap()

    x_fm = din("x_fm", (SPC, DM, L), F32)
    w_inT = din("w_inT", (NL, DM, 2 * DI), F32R)
    w_xpT = din("w_xpT", (NL, DI, 96), F32R)
    w_dtpT = din("w_dtpT", (NL, DR, DI), F32R)
    w_outT = din("w_outT", (NL, DI, DM), F32R)
    w_f1T = din("w_f1T", (DM, DH), F32)
    w_f2T = din("w_f2T", (DM, DH), F32)
    w_h1T = din("w_h1T", (DH, DH), F32)
    w_h2T = din("w_h2T", (DH, DH), F32)
    w_f3T = din("w_f3T", (DH, DM), F32)
    norm_w = din("norm_w", (NL, 128, NDB), F32)
    norm_b = din("norm_b", (NL, 128, NDB), F32)
    conv_w = din("conv_w", (NL, 128, NIB * DC), F32)
    conv_b = din("conv_b", (NL, 128, NIB), F32)
    dtp_b = din("dtp_b", (NL, 128, NIB), F32)
    Dp = din("Dp", (NL, 128, NIB), F32)
    A_log = din("A_log", (NL, 128, NIB * DS), F32)
    normf_w = din("normf_w", (128, NDB), F32)
    normf_b = din("normf_b", (128, NDB), F32)
    f1s = din("f1s", (128, NHB), F32)
    f2s = din("f2s", (128, NHB), F32)
    h1s = din("h1s", (128, NHB), F32)
    h2s = din("h2s", (128, NHB), F32)
    f3s = din("f3s", (128, NDB), F32)
    sel = din("sel", (DS, DS * 128), F32R)
    ones128 = din("ones128", (128, 1), F32)

    out_fm = nc.dram_tensor("out_fm", [SPC, DM, L], F32, kind="ExternalOutput").ap()

    def scr(name, shape, dt):
        return nc.dram_tensor(name, list(shape), dt, kind="Internal").ap()

    sp_xn_r = scr("sp_xn_r", (NDB, 128, L), F32)
    sp_xn_b = scr("sp_xn_b", (NDB, 128, L), BF16)
    sp_r = scr("sp_r", (NDB, 128, L), F32)
    sp_z = scr("sp_z", (NIB, 128, L), BF16)
    sp_xc = scr("sp_xc", (NIB, 128, L), F32R)
    sp_dtr = scr("sp_dtr", (NIB, 128, L), F32)
    sp_y = scr("sp_y", (NIB, 128, L), F32)
    sp_hg3 = scr("sp_hg3", (NHB, 128, L), BF16)

    dbg_t = {}
    es = ExitStack()

    _uid = [0]
    _sb_init = [None]

    def sb(name, shape, dt, st=None):
        stack = st if st is not None else es
        _uid[0] += 1
        if _sb_init[0] is None:
            _sb_init[0] = nc.sbuf_bytes_remaining
        before = nc.sbuf_bytes_remaining
        h = stack.enter_context(
            nc.sbuf_tensor(f"{name}_{_uid[0]}", list(shape), dt))
        after = nc.sbuf_bytes_remaining
        start = _sb_init[0] - before
        R.sbuf_addr[h.name] = (start, start + (before - after))
        return h

    sems = {}
    for sn in ("dS", "dG", "sP", "sA", "sV", "sO", "bar"):
        sems[sn] = SemT(es.enter_context(nc.semaphore(sn)), sn)
    for i in range(8):
        sems[f"dSp{i}"] = SemT(es.enter_context(nc.semaphore(f"dSp{i}")),
                               f"dSp{i}")
    for i in range(6):
        sems[f"dGp{i}"] = SemT(es.enter_context(nc.semaphore(f"dGp{i}")),
                               f"dGp{i}")
    dS, dG, sP, sA, sV, sO = (sems[k] for k in ("dS", "dG", "sP", "sA", "sV", "sO"))
    R = Rec(sems, serial=serial)
    R.pools = {"dS": [sems[f"dSp{i}"] for i in range(8)],
               "dG": [sems[f"dGp{i}"] for i in range(6)]}
    R.rr = {"dS": 0, "dG": 0}
    if audit:
        AUDIT["acc"] = R.audit = []

    def dbg_dump(tag, src_ap, shape, dt, lvl=1):
        if not dbg or lvl > int(dbg):
            return
        t = nc.dram_tensor(f"dbg_{tag}", list(shape), dt,
                           kind="ExternalOutput").ap()
        dbg_t[tag] = t
        R.emit("sync", "dma_start", out=t, in_=src_ap, inc=dS, incn=16)
        R.barrier()

    p_nw = sb("p_nw", (128, NL * NDB), F32)
    p_nb = sb("p_nb", (128, NL * NDB), F32)
    p_cw = sb("p_cw", (128, NL * NIB * DC), F32)
    p_cb = sb("p_cb", (128, NL * NIB), F32)
    p_dtb = sb("p_dtb", (128, NL * NIB), F32)
    p_D = sb("p_D", (128, NL * NIB), F32)
    p_An = sb("p_An", (128, NL * NIB * DS), F32)
    p_nfw = sb("p_nfw", (128, NDB), F32)
    p_nfb = sb("p_nfb", (128, NDB), F32)
    p_f1s = sb("p_f1s", (128, NHB), F32)
    p_f2s = sb("p_f2s", (128, NHB), F32)
    p_h1s = sb("p_h1s", (128, NHB), F32)
    p_h2s = sb("p_h2s", (128, NHB), F32)
    p_f3s = sb("p_f3s", (128, NDB), F32)
    p_sel = sb("p_sel", (DS, DS * 128), F32R)
    p_o128 = sb("p_o128", (128, 1), F32)
    Bc = sb("Bc", (DS, L), F32R)
    Cc = sb("Cc", (DS, L), F32R)
    lnrow = sb("lnrow", (1, 4 * TC), F32)
    lnrow_r = sb("lnrow_r", (1, 2 * TC), F32R)
    lnsq = sb("lnsq", (128, TC), F32)
    p_eps = sb("p_eps", (1, 1), F32)
    carry = sb("carry", (128, GD * DS), F32)

    pbk = [es.enter_context(nc.psum_tensor(f"pb{i}", [128, 512], F32))
           for i in range(8)]

    for dt_t, src in ((p_nw, norm_w), (p_nb, norm_b), (p_cb, conv_b),
                      (p_dtb, dtp_b), (p_D, Dp), (p_nfw, normf_w),
                      (p_nfb, normf_b), (p_f1s, f1s), (p_f2s, f2s),
                      (p_h1s, h1s), (p_h2s, h2s), (p_f3s, f3s),
                      (p_o128, ones128)):
        sh = src.shape
        if len(sh) == 3:
            R.emit("sync", "dma_start",
                   out=dt_t.ap().rearrange("p (l k) -> p l k", l=sh[0]),
                   in_=src.rearrange("l p k -> p l k"), inc=dS, incn=16)
        else:
            R.emit("sync", "dma_start", out=dt_t[:, :], in_=src, inc=dS, incn=16)
    R.emit("sync", "dma_start",
           out=p_cw.ap().rearrange("p (l k) -> p l k", l=NL),
           in_=conv_w.rearrange("l p k -> p l k"), inc=dS, incn=16)
    mal = R.emit("sync", "dma_start",
                 out=p_An.ap().rearrange("p (l k) -> p l k", l=NL),
                 in_=A_log.rearrange("l p k -> p l k"), inc=dS, incn=16)
    R.emit("sync", "dma_start", out=p_sel[:, :], in_=sel, inc=dS, incn=16)
    mA = R.emit("scalar", "activation", out=p_An[:, :], in_=p_An[:, :],
                func=AF.Exp, waits=[mal], inc=sA)
    R.emit("vector", "memset", p_eps[:, :], EPS)
    R.emit("vector", "tensor_single_scalar", out=p_An[:, :], in_=p_An[:, :],
           scalar=-1.0, op=ALU.mult, waits=[mA], inc=sV)
    R.barrier()

    def ln_stats(src_i, nblk, bank_s, bank_b, bank_q, waits, out_i,
                 affine=None, extra_out=None):
        w0 = list(waits)
        mm = None
        for i in range(nblk):
            mm = R.emit("tensor", "matmul", pbk[bank_s][0:1, 0:TC],
                        lhsT=p_o128[:, :], rhs=src_i(i),
                        start=(i == 0), stop=(i == nblk - 1),
                        waits=w0 if i == 0 else ())
        msq = None
        mlast = None
        for i in range(nblk):
            msq = R.emit("scalar", "activation", out=lnsq[:, :], in_=src_i(i),
                         func=AF.Square,
                         waits=(w0 + [mlast]) if i == 0 else [mlast], inc=sA)
            mlast = R.emit("tensor", "matmul", pbk[bank_q][0:1, 0:TC],
                           lhsT=p_o128[:, :], rhs=lnsq[:, :],
                           start=(i == 0), stop=(i == nblk - 1),
                           waits=[msq], inc=sP)
        mst = mlast
        mv = R.emit("vector", "tensor_copy", out=lnrow[:, 0:TC],
                    in_=pbk[bank_s][0:1, 0:TC], waits=[mst] + w0, inc=sV)
        mv = R.emit("vector", "tensor_tensor", out=lnrow[:, TC:2 * TC],
                    in0=lnrow[:, 0:TC], in1=lnrow[:, 0:TC], op=ALU.mult,
                    waits=[mv], inc=sV)
        mv = R.emit("vector", "tensor_tensor", out=lnrow[:, 2 * TC:3 * TC],
                    in0=pbk[bank_q][0:1, 0:TC], in1=lnrow[:, TC:2 * TC],
                    op=ALU.subtract, waits=[mv], inc=sV)
        ma = R.emit("scalar", "activation", out=lnrow[:, 3 * TC:4 * TC],
                    in_=lnrow[:, 2 * TC:3 * TC], func=AF.Ln,
                    bias=p_eps[0:1, 0:1], waits=[mv], inc=sA)
        ma = R.emit("scalar", "activation", out=lnrow_r[:, TC:2 * TC],
                    in_=lnrow[:, 3 * TC:4 * TC], func=AF.Exp, scale=-0.5,
                    waits=[ma], inc=sA)
        mv2 = R.emit("vector", "tensor_copy", out=lnrow_r[:, 0:TC],
                     in_=lnrow[:, 0:TC], waits=[mv], inc=sV)
        mb1 = R.emit("tensor", "matmul", pbk[bank_b][:, 0:TC],
                     lhsT=p_sel[0:1, 0:128], rhs=lnrow_r[:, 0:TC],
                     start=True, stop=True, waits=[mv2] + w0, inc=sP)
        mb2 = R.emit("tensor", "matmul", pbk[bank_b][:, TC:2 * TC],
                     lhsT=p_sel[0:1, 0:128], rhs=lnrow_r[:, TC:2 * TC],
                     start=True, stop=True, waits=[ma], inc=sP)
        last = None
        for i in range(nblk):
            m1 = R.emit("vector", "tensor_tensor", out=lnsq[:, :],
                        in0=src_i(i), in1=pbk[bank_b][:, 0:TC],
                        op=ALU.subtract, waits=[mb1, mb2], inc=sV)
            if affine is None:
                last = R.emit("vector", "tensor_tensor", out=out_i(i),
                              in0=lnsq[:, :], in1=pbk[bank_b][:, TC:2 * TC],
                              op=ALU.mult, waits=[mb2, m1], inc=sV)
                if extra_out is not None:
                    last = R.emit("vector", "tensor_copy", out=extra_out(i),
                                  in_=out_i(i), waits=[last], inc=sV)
            else:
                w_ap, b_ap = affine
                m2 = R.emit("vector", "scalar_tensor_tensor", out=lnsq[:, :],
                            in0=lnsq[:, :], scalar=w_ap(i),
                            in1=pbk[bank_b][:, TC:2 * TC],
                            op0=ALU.mult, op1=ALU.mult, waits=[mb2, m1], inc=sV)
                last = R.emit("vector", "tensor_single_scalar", out=out_i(i),
                              in_=lnsq[:, :], scalar=b_ap(i), op=ALU.add,
                              waits=[m2], inc=sV)
        return last

    for s in range(SPC):
        xs_t = x_fm[s].rearrange("(b p) t -> b p t", p=128)

        R.stage = "P1"
        with ExitStack() as st1:
            p1_x = sb("p1_x", (128, NDB * TC), F32, st1)
            p1_o = sb("p1_o", (128, NDB * TC), F32, st1)
            p1_ob = sb("p1_ob", (128, NDB * TC), BF16, st1)
            prevst = [None, None, None]
            for c in range(NCH):
                cl = slice(c * TC, (c + 1) * TC)
                mld = R.emit("sync", "dma_start",
                             out=p1_x.ap().rearrange("p (b t) -> p b t", b=NDB),
                             in_=xs_t[:, :, cl].rearrange("b p t -> p b t"),
                             waits=[R.now(sV)], inc=dS, incn=16)
                src_i = lambda i: p1_x[:, i * TC:(i + 1) * TC]
                out_i = lambda i: p1_o[:, i * TC:(i + 1) * TC]
                ex_i = lambda i: p1_ob[:, i * TC:(i + 1) * TC]
                last = ln_stats(src_i, NDB, 0, 1, 2,
                                [mld] + [m for m in prevst if m], out_i,
                                affine=None, extra_out=ex_i)
                prevst = [
                    R.emit("sync", "dma_start", out=sp_xn_r[:, :, cl].rearrange("b p t -> p b t"),
                           in_=p1_o.ap().rearrange("p (b t) -> p b t", b=NDB),
                           waits=[last], inc=dS, incn=16),
                    R.emit("sync", "dma_start", out=sp_xn_b[:, :, cl].rearrange("b p t -> p b t"),
                           in_=p1_ob.ap().rearrange("p (b t) -> p b t", b=NDB),
                           waits=[last], inc=dS, incn=16),
                    last]
            R.barrier()
        if s == 0:
            dbg_dump("xn_r", sp_xn_r[:, :, :], (NDB, 128, L), F32)

        for l in range(NL):
            r_src = sp_xn_r if l == 0 else sp_r

            R.stage = "A0"
            with ExitStack() as stA:
                hid = sb("hid", (128, NDB * L), F32R, stA)
                a0x = sb("a0x", (128, NDB * TC), F32, stA)
                prev_last = None
                for c in range(NCH):
                    cl = slice(c * TC, (c + 1) * TC)
                    mld = R.emit("sync", "dma_start",
                                 out=a0x.ap().rearrange("p (b t) -> p b t", b=NDB),
                                 in_=r_src[:, :, cl].rearrange("b p t -> p b t"),
                                 waits=[R.now(sV)], inc=dS, incn=16)
                    src_i = lambda i: a0x[:, i * TC:(i + 1) * TC]
                    out_i = (lambda i, c=c:
                             hid[:, i * L + c * TC: i * L + (c + 1) * TC])
                    prev_last = ln_stats(
                        src_i, NDB, 0, 1, 2,
                        [mld] + ([prev_last] if prev_last else []), out_i,
                        affine=(lambda i, l=l: p_nw[:, l * NDB + i:l * NDB + i + 1],
                                lambda i, l=l: p_nb[:, l * NDB + i:l * NDB + i + 1]))
                R.barrier()

                R.stage = "A1"
                ring_in = sb("ring_in", (128, 4 * NDB * 128), F32R, stA)
                cstg = sb("cstg", (128, 3 * (L + DC - 1)), F32, stA)
                cacc = sb("cacc", (128, L), F32, stA)
                xcst = sb("xcst", (128, 2 * L), F32R, stA)
                zst = sb("zst", (128, 2 * L), BF16, stA)
                NQ = L // 512
                ring_m = [None] * 4
                evm = {}
                msl_prev = None
                for eb in range(2 * NIB):
                    sl_i = eb % 4
                    ring_m[sl_i] = R.emit(
                        "sync", "dma_start",
                        out=ring_in[:, sl_i * 1024:(sl_i + 1) * 1024]
                            .rearrange("p (k e) -> p k e", k=NDB),
                        in_=w_inT[l, :, eb * 128:(eb + 1) * 128]
                            .rearrange("(k p) e -> p k e", p=128),
                        waits=[evm.get(("pe", eb - 4))], inc=dS, incn=16)
                    bank0 = 0 if eb % 2 == 0 else 4
                    mmm = None
                    for k in range(NDB):
                        for q in range(NQ):
                            mmm = R.emit(
                                "tensor", "matmul", pbk[bank0 + q][:, :],
                                lhsT=ring_in[:, sl_i * 1024 + k * 128:
                                             sl_i * 1024 + (k + 1) * 128],
                                rhs=hid[:, k * L + q * 512:k * L + (q + 1) * 512],
                                start=(k == 0), stop=(k == NDB - 1),
                                waits=([ring_m[sl_i], evm.get(("ev", eb - 2))]
                                       if (k == 0 and q == 0) else ()),
                                inc=sP if (k == NDB - 1 and q == NQ - 1) else None)
                    evm[("pe", eb)] = mmm
                    if eb < NIB:
                        db = eb
                        st_i = db % 3
                        stg = cstg[:, st_i * (L + 3):(st_i + 1) * (L + 3)]
                        R.emit("vector", "memset", stg[:, 0:3], 0.0,
                               waits=[mmm])
                        mev = None
                        for q in range(NQ):
                            mev = R.emit("vector", "tensor_copy",
                                         out=stg[:, 3 + q * 512:3 + (q + 1) * 512],
                                         in_=pbk[bank0 + q][:, :],
                                         waits=[mmm], inc=sV)
                        evm[("ev", eb)] = mev
                        cwb = (lambda k, l=l, db=db:
                               p_cw[:, l * NIB * DC + db * DC + k:
                                    l * NIB * DC + db * DC + k + 1])
                        mcv = R.emit("vector", "tensor_scalar", out=cacc[:, :],
                                     in0=stg[:, 0:L], scalar1=cwb(0),
                                     scalar2=None, op0=ALU.mult,
                                     waits=[mev, msl_prev], inc=sV)
                        for k in range(1, DC):
                            mcv = R.emit("vector", "scalar_tensor_tensor",
                                         out=cacc[:, :], in0=stg[:, k:k + L],
                                         scalar=cwb(k), in1=cacc[:, :],
                                         op0=ALU.mult, op1=ALU.add,
                                         waits=[mcv], inc=sV)
                        xst_i = db % 2
                        xstg = xcst[:, xst_i * L:(xst_i + 1) * L]
                        msl = R.emit("scalar", "activation", out=xstg,
                                     in_=cacc[:, :], func=AF.Silu,
                                     bias=p_cb[:, l * NIB + db:l * NIB + db + 1],
                                     waits=[mcv, evm.get(("xcd", db - 2))],
                                     inc=sA)
                        msl_prev = msl
                        evm[("xcd", db)] = R.emit(
                            "sync", "dma_start", out=sp_xc[db], in_=xstg,
                            waits=[msl], inc=dS, incn=16)
                    else:
                        db = eb - NIB
                        zst_i = db % 2
                        zstg = zst[:, zst_i * L:(zst_i + 1) * L]
                        mev = None
                        for q in range(NQ):
                            mev = R.emit("vector", "tensor_copy",
                                         out=zstg[:, q * 512:(q + 1) * 512],
                                         in_=pbk[bank0 + q][:, :],
                                         waits=[mmm, evm.get(("zd", db - 2))],
                                         inc=sV)
                        evm[("ev", eb)] = mev
                        evm[("zd", db)] = R.emit(
                            "sync", "dma_start", out=sp_z[db], in_=zstg,
                            waits=[mev], inc=dS, incn=16)
                R.barrier()
                if s == 0 and l == 0 and dbg:
                    dbg_dump("hid", hid.ap(), (128, NDB * L), F32R, 2)
                    dbg_dump("xc", sp_xc[:, :, :], (NIB, 128, L), F32R, 2)
                    dbg_dump("z", sp_z[:, :, :], (NIB, 128, L), BF16, 2)

            R.stage = "A3"
            with ExitStack() as st3:
                w_xp_sb = sb("w_xp_sb", (128, NIB * 96), F32R, st3)
                w_dtp_sb = sb("w_dtp_sb", (DR, DI), F32R, st3)
                xcc = sb("xcc", (128, NIB * TC), F32R, st3)
                dtr_sb = sb("dtr_sb", (DR, 2 * TC), F32R, st3)
                dtst = sb("dtst", (128, NIB * TC), F32, st3)
                mxp = R.emit("sync", "dma_start",
                             out=w_xp_sb.ap().rearrange("p (k c) -> p k c", k=NIB),
                             in_=w_xpT[l].rearrange("(k p) c -> p k c", p=128),
                             inc=dS, incn=16)
                mdtp = R.emit("sync", "dma_start", out=w_dtp_sb[:, :],
                              in_=w_dtpT[l], inc=dS, incn=16)
                prev_mC = None
                prev_dma = None
                dtev = {}
                for c in range(NCH):
                    cl = slice(c * TC, (c + 1) * TC)
                    mxc = R.emit("sync", "dma_start",
                                 out=xcc.ap().rearrange("p (k t) -> p k t", k=NIB),
                                 in_=sp_xc[:, :, cl].rearrange("k p t -> p k t"),
                                 waits=[prev_mC], inc=dS, incn=16)
                    m_dtr = m_B = m_C = None
                    for k in range(NIB):
                        m_dtr = R.emit("tensor", "matmul", pbk[6][0:64, 0:TC],
                                       lhsT=w_xp_sb[:, k * 96:k * 96 + 64],
                                       rhs=xcc[:, k * TC:(k + 1) * TC],
                                       start=(k == 0), stop=(k == NIB - 1),
                                       waits=([mxc, mxp, R.now(sV)]
                                              if k == 0 else ()),
                                       inc=sP if k == NIB - 1 else None)
                    for k in range(NIB):
                        m_B = R.emit("tensor", "matmul", pbk[7][0:16, 0:TC],
                                     lhsT=w_xp_sb[:, k * 96 + 64:k * 96 + 80],
                                     rhs=xcc[:, k * TC:(k + 1) * TC],
                                     start=(k == 0), stop=(k == NIB - 1),
                                     inc=sP if k == NIB - 1 else None)
                    for k in range(NIB):
                        m_C = R.emit("tensor", "matmul", pbk[4][0:16, 0:TC],
                                     lhsT=w_xp_sb[:, k * 96 + 80:k * 96 + 96],
                                     rhs=xcc[:, k * TC:(k + 1) * TC],
                                     start=(k == 0), stop=(k == NIB - 1),
                                     inc=sP if k == NIB - 1 else None)
                    prev_mC = m_C
                    mBe = R.emit("vector", "tensor_copy", out=Bc[:, cl],
                                 in_=pbk[7][0:16, 0:TC], waits=[m_B], inc=sV)
                    mCe = R.emit("vector", "tensor_copy", out=Cc[:, cl],
                                 in_=pbk[4][0:16, 0:TC], waits=[m_C], inc=sV)
                    dt_i = c % 2
                    mDe = R.emit("vector", "tensor_copy",
                                 out=dtr_sb[:, dt_i * TC:(dt_i + 1) * TC],
                                 in_=pbk[6][0:64, 0:TC], waits=[m_dtr], inc=sV)
                    wA3 = R.now(sA)
                    for odb in range(NIB):
                        bk = 2 + odb % 2
                        mdt = R.emit("tensor", "matmul", pbk[bk][:, 0:TC],
                                     lhsT=w_dtp_sb[:, odb * 128:(odb + 1) * 128],
                                     rhs=dtr_sb[:, dt_i * TC:(dt_i + 1) * TC],
                                     start=True, stop=True,
                                     waits=[mDe, mdtp,
                                            dtev.get(odb - 2, wA3)],
                                     inc=sP)
                        dtev[odb] = R.emit(
                            "scalar", "activation",
                            out=dtst[:, odb * TC:(odb + 1) * TC],
                            in_=pbk[bk][:, 0:TC], func=AF.Identity,
                            bias=p_dtb[:, l * NIB + odb:l * NIB + odb + 1],
                            waits=[mdt, prev_dma], inc=sA)
                    prev_dma = R.emit(
                        "sync", "dma_start", out=sp_dtr[:, :, cl].rearrange("k p t -> p k t"),
                        in_=dtst.ap().rearrange("p (k t) -> p k t", k=NIB),
                        waits=[dtev[o] for o in range(NIB)], inc=dS, incn=16)
                    dtev = {}
                R.barrier()
                if s == 0 and l == 0 and dbg:
                    dbg_dump("dtr", sp_dtr[:, :, :], (NIB, 128, L), F32, 2)
                    dbg_dump("Bc", Bc[:, :], (DS, L), F32R, 2)
                    dbg_dump("Cc", Cc[:, :], (DS, L), F32R, 2)

            R.stage = "B"
            with ExitStack() as stB:
                bs_dt = sb("bs_dt", (128, GD * TH), F32, stB)
                bs_du = sb("bs_du", (128, GD * TH), F32, stB)
                bs_y = sb("bs_y", (128, GD * TH), F32, stB)
                bs_xc = sb("bs_xc", (128, 2 * TH), F32R, stB)
                slot = sb("slot", (128, 6 * TH), F32, stB)
                SL = [slot[:, i * TH:(i + 1) * TH] for i in range(6)]
                SA = [SL[0], SL[1]]
                SB_ = SL[2]
                SH = SL[3]
                SP_ = [SL[4], SL[5]]
                XS = [bs_xc[:, i * TH:(i + 1) * TH] for i in range(2)]
                BK = [(4, 5, 6, 7), (0, 1, 2, 3)]
                cons = {0: None, 1: None}
                last_spill = None
                msc_h = {}
                my_h = {}
                for g in range(NG):
                    for h in range(2):
                        hl = slice(h * TH, (h + 1) * TH)
                        mdl = R.emit("sync", "dma_start",
                                     out=bs_du.ap()
                                         .rearrange("p (j t) -> p j t", j=GD),
                                     in_=sp_dtr[g * GD:(g + 1) * GD, :, hl]
                                         .rearrange("j p t -> p j t"),
                                     waits=[R.now(sV)], inc=dS, incn=16)
                        m_addp = None
                        mxl = [None, None]
                        mdu = {}
                        for j in range(GD):
                            raw = bs_du[:, j * TH:(j + 1) * TH]
                            m1 = R.emit("scalar", "activation", out=SL[0],
                                        in_=raw, func=AF.Abs,
                                        waits=([mdl, R.now(sV), R.now(sO)]
                                               if j == 0 else [mdl, m_addp]),
                                        inc=sA)
                            m1 = R.emit("scalar", "activation", out=SL[2],
                                        in_=SL[0], func=AF.Exp, scale=-1.0,
                                        waits=[m1], inc=sA)
                            m1 = R.emit("scalar", "activation", out=SL[0],
                                        in_=SL[2], func=AF.Ln, bias=1.0,
                                        waits=[m1], inc=sA)
                            m2 = R.emit("vector", "tensor_single_scalar",
                                        out=SL[3], in_=raw, scalar=0.0,
                                        op=ALU.max, waits=[mdl], inc=sV)
                            m_addp = R.emit("vector", "tensor_tensor",
                                            out=bs_dt[:, j * TH:(j + 1) * TH],
                                            in0=SL[0], in1=SL[3], op=ALU.add,
                                            waits=[m1, m2], inc=sV)
                            pp = j % 2
                            mxl[pp] = R.emit(
                                "sync", "dma_start", out=XS[pp],
                                in_=sp_xc[g * GD + j, :, hl],
                                waits=[m_addp if j == 0 else None,
                                       mdu.get(j - 2)], inc=dS, incn=16)
                            mdu[j] = R.emit("vector", "tensor_tensor",
                                            out=bs_du[:, j * TH:(j + 1) * TH],
                                            in0=bs_dt[:, j * TH:(j + 1) * TH],
                                            in1=XS[pp], op=ALU.mult,
                                            waits=[mxl[pp]], inc=sV)
                        mdu_done = R.now(sV)
                        mB2 = {}
                        mC2 = {}
                        for n in range(DS):
                            par = n % 2
                            b0, b1, c0, c1 = BK[par]
                            wb = [cons[par], mdu_done]
                            R.emit("tensor", "matmul", pbk[b0][:, :],
                                   lhsT=p_sel[:, n * 128:(n + 1) * 128],
                                   rhs=Bc[:, h * TH:h * TH + 512],
                                   start=True, stop=True, waits=wb)
                            mB2[n] = R.emit("tensor", "matmul", pbk[b1][:, :],
                                            lhsT=p_sel[:, n * 128:(n + 1) * 128],
                                            rhs=Bc[:, h * TH + 512:(h + 1) * TH],
                                            start=True, stop=True, inc=sP)
                            R.emit("tensor", "matmul", pbk[c0][:, :],
                                   lhsT=p_sel[:, n * 128:(n + 1) * 128],
                                   rhs=Cc[:, h * TH:h * TH + 512],
                                   start=True, stop=True)
                            mC2[n] = R.emit("tensor", "matmul", pbk[c1][:, :],
                                            lhsT=p_sel[:, n * 128:(n + 1) * 128],
                                            rhs=Cc[:, h * TH + 512:(h + 1) * TH],
                                            start=True, stop=True, inc=sP)
                            mbl = mpl = None
                            for j in range(GD):
                                it = n * GD + j
                                db = g * GD + j
                                ma = R.emit(
                                    "scalar", "activation", out=SA[it % 2],
                                    in_=bs_dt[:, j * TH:(j + 1) * TH],
                                    func=AF.Exp,
                                    scale=p_An[:, l * NIB * DS + db * DS + n:
                                               l * NIB * DS + db * DS + n + 1],
                                    waits=[msc_h.get(it - 2, mdu_done)
                                           if it >= 2 else mdu_done],
                                    inc=sA)
                                R.emit("vector", "tensor_tensor",
                                       out=SB_[:, 0:512],
                                       in0=bs_du[:, j * TH:j * TH + 512],
                                       in1=pbk[b0][:, :], op=ALU.mult,
                                       waits=[mB2[n]], inc=sV)
                                mbl = R.emit("vector", "tensor_tensor",
                                             out=SB_[:, 512:1024],
                                             in0=bs_du[:, j * TH + 512:
                                                       (j + 1) * TH],
                                             in1=pbk[b1][:, :], op=ALU.mult,
                                             inc=sV)
                                init = (0.0 if h == 0 else
                                        carry[:, j * DS + n:j * DS + n + 1])
                                msc = R.emit("vector", "tensor_tensor_scan",
                                             out=SH, data0=SA[it % 2],
                                             data1=SB_, initial=init,
                                             op0=ALU.mult, op1=ALU.add,
                                             waits=[ma], inc=sV)
                                msc_h[it] = msc
                                if h == 0:
                                    R.emit("vector", "tensor_copy",
                                           out=carry[:, j * DS + n:
                                                     j * DS + n + 1],
                                           in_=SH[:, TH - 1:TH], inc=sV)
                                mp1 = R.emit("vector", "tensor_tensor",
                                             out=SP_[it % 2][:, 0:512],
                                             in0=SH[:, 0:512],
                                             in1=pbk[c0][:, :], op=ALU.mult,
                                             waits=[mC2[n],
                                                    my_h.get(it - 2)],
                                             inc=sV)
                                mpl = R.emit("vector", "tensor_tensor",
                                             out=SP_[it % 2][:, 512:1024],
                                             in0=SH[:, 512:1024],
                                             in1=pbk[c1][:, :], op=ALU.mult,
                                             inc=sV)
                                if n == 0:
                                    my = R.emit("gpsimd", "tensor_copy",
                                                out=bs_y[:, j * TH:(j + 1) * TH],
                                                in_=SP_[it % 2],
                                                waits=[mp1, mpl, last_spill],
                                                inc=sO)
                                else:
                                    my = R.emit("gpsimd", "tensor_tensor",
                                                out=bs_y[:, j * TH:(j + 1) * TH],
                                                in0=bs_y[:, j * TH:(j + 1) * TH],
                                                in1=SP_[it % 2], op=ALU.add,
                                                waits=[mp1, mpl], inc=sO)
                                my_h[it] = my
                            cons[par] = mpl
                        itmax = DS * GD
                        msc_h = {k - itmax: v for k, v in msc_h.items()
                                 if k >= itmax - 2}
                        my_h = {k - itmax: v for k, v in my_h.items()
                                if k >= itmax - 2}
                        last_spill = R.emit(
                            "sync", "dma_start",
                            out=sp_y[g * GD:(g + 1) * GD, :, hl]
                                .rearrange("j p t -> p j t"),
                            in_=bs_y.ap().rearrange("p (j t) -> p j t", j=GD),
                            waits=[my_h.get(-1), my_h.get(-2)],
                            inc=dS, incn=16)
                R.barrier()

            R.stage = "C"
            with ExitStack() as stC:
                c_y = sb("c_y", (128, NIB * TC), F32, stC)
                c_xc = sb("c_xc", (128, NIB * TC), F32R, stC)
                c_z = sb("c_z", (128, NIB * TC), BF16, stC)
                c_zs = sb("c_zs", (128, TC), F32, stC)
                c_yg = sb("c_yg", (128, NIB * TC), F32R, stC)
                c_r = sb("c_r", (128, NDB * TC), F32, stC)
                ring_op = sb("ring_op", (128, 2 * NIB * 128), F32R, stC)
                rop_m = [None, None]
                opev = {}
                for c in range(NCH):
                    cl = slice(c * TC, (c + 1) * TC)
                    wpre = [R.now(sV), R.now(sA), R.now(sP)]
                    mly = R.emit("sync", "dma_start",
                                 out=c_y.ap().rearrange("p (k t) -> p k t", k=NIB),
                                 in_=sp_y[:, :, cl].rearrange("k p t -> p k t"), waits=wpre, inc=dS, incn=16)
                    mlx = R.emit("sync", "dma_start",
                                 out=c_xc.ap().rearrange("p (k t) -> p k t", k=NIB),
                                 in_=sp_xc[:, :, cl].rearrange("k p t -> p k t"), inc=dS, incn=16)
                    mlz = R.emit("sync", "dma_start",
                                 out=c_z.ap().rearrange("p (k t) -> p k t", k=NIB),
                                 in_=sp_z[:, :, cl].rearrange("k p t -> p k t"), inc=dS, incn=16)
                    mlr = R.emit("sync", "dma_start",
                                 out=c_r.ap().rearrange("p (k t) -> p k t", k=NDB),
                                 in_=r_src[:, :, cl].rearrange("b p t -> p b t"),
                                 waits=[R.now(sV)], inc=dS, incn=16)
                    mgate = None
                    for db in range(NIB):
                        ts_ = slice(db * TC, (db + 1) * TC)
                        m1 = R.emit("vector", "scalar_tensor_tensor",
                                    out=c_y[:, ts_], in0=c_xc[:, ts_],
                                    scalar=p_D[:, l * NIB + db:l * NIB + db + 1],
                                    in1=c_y[:, ts_], op0=ALU.mult, op1=ALU.add,
                                    waits=[mly, mlx], inc=sV)
                        m2 = R.emit("scalar", "activation", out=c_zs[:, :],
                                    in_=c_z[:, ts_], func=AF.Silu,
                                    waits=[mlz, mgate], inc=sA)
                        mgate = R.emit("vector", "tensor_tensor",
                                       out=c_yg[:, ts_], in0=c_y[:, ts_],
                                       in1=c_zs[:, :], op=ALU.mult,
                                       waits=[m2, m1], inc=sV)
                    for odb in range(NDB):
                        sl_i = odb % 2
                        rop_m[sl_i] = R.emit(
                            "sync", "dma_start",
                            out=ring_op[:, sl_i * 2048:(sl_i + 1) * 2048]
                                .rearrange("p (k e) -> p k e", k=NIB),
                            in_=w_outT[l, :, odb * 128:(odb + 1) * 128]
                                .rearrange("(k p) e -> p k e", p=128),
                            waits=[opev.get((c, odb - 2))], inc=dS, incn=16)
                        bk = odb % 2
                        mmo = None
                        for k in range(NIB):
                            mmo = R.emit(
                                "tensor", "matmul", pbk[bk][:, 0:TC],
                                lhsT=ring_op[:, sl_i * 2048 + k * 128:
                                             sl_i * 2048 + (k + 1) * 128],
                                rhs=c_yg[:, k * TC:(k + 1) * TC],
                                start=(k == 0), stop=(k == NIB - 1),
                                waits=([mgate, rop_m[sl_i],
                                        opev.get((c, odb - 2)),
                                        opev.get((c - 1, NDB - 2 + odb % 2))]
                                       if k == 0 else ()),
                                inc=sP if k == NIB - 1 else None)
                        opev[(c, odb)] = R.emit(
                            "vector", "tensor_tensor",
                            out=c_r[:, odb * TC:(odb + 1) * TC],
                            in0=c_r[:, odb * TC:(odb + 1) * TC],
                            in1=pbk[bk][:, 0:TC], op=ALU.add,
                            waits=[mmo, mlr], inc=sV)
                    R.emit("sync", "dma_start", out=sp_r[:, :, cl].rearrange("b p t -> p b t"),
                           in_=c_r.ap().rearrange("p (k t) -> p k t", k=NDB),
                           waits=[opev[(c, o)] for o in range(NDB)],
                           inc=dS, incn=16)
                R.barrier()
                if s == 0 and l == 0 and dbg:
                    dbg_dump("r1", sp_r[:, :, :], (NDB, 128, L), F32, 2)

        R.stage = "FFN"
        with ExitStack() as stF:
            fx = sb("fx", (128, NDB * TF), BF16, stF)
            fhg = sb("fhg", (128, NHB * TF), BF16, stF)
            fhg2 = sb("fhg2", (128, NHB * TF), BF16, stF)
            fw12 = sb("fw12", (128, 4 * NDB * 128), BF16, stF)
            fwh = sb("fwh", (128, 2 * NHB * 128), BF16, stF)
            ftmp = sb("ftmp", (128, TF), F32, stF)
            fst = sb("fst", (128, 2 * TF), BF16, stF)
            for ft in range(NFT):
                fl = slice(ft * TF, (ft + 1) * TF)
                mfx = R.emit("gpsimd", "dma_start",
                             out=fx.ap().rearrange("p (k t) -> p k t", k=NDB),
                             in_=sp_xn_b[:, :, fl].rearrange("b p t -> p b t"),
                             waits=[R.now(sP)], inc=dG, incn=16)
                ev = {}
                fr_m = [None] * 4
                for hb in range(NHB):
                    for w_i, wsrc in enumerate((w_f1T, w_f2T)):
                        it = 2 * hb + w_i
                        sl_i = it % 4
                        fr_m[sl_i] = R.emit(
                            "gpsimd", "dma_start",
                            out=fw12[:, sl_i * 1024:(sl_i + 1) * 1024]
                                .rearrange("p (k e) -> p k e", k=NDB),
                            in_=wsrc[:, hb * 128:(hb + 1) * 128]
                                .rearrange("(k p) e -> p k e", p=128),
                            waits=[ev.get(("c", it - 4))], inc=dG, incn=16)
                        bk = it % 2
                        mmf = None
                        for k in range(NDB):
                            mmf = R.emit(
                                "tensor", "matmul", pbk[bk][:, :],
                                lhsT=fw12[:, sl_i * 1024 + k * 128:
                                          sl_i * 1024 + (k + 1) * 128],
                                rhs=fx[:, k * TF:(k + 1) * TF],
                                start=(k == 0), stop=(k == NDB - 1),
                                waits=([mfx, fr_m[sl_i], ev.get(("c", it - 2))]
                                       if k == 0 else ()),
                                inc=sP if k == NDB - 1 else None)
                        if w_i == 0:
                            m1 = R.emit("scalar", "activation", out=ftmp[:, :],
                                        in_=pbk[bk][:, :], func=AF.Silu,
                                        scale=p_f1s[:, hb:hb + 1],
                                        waits=[mmf, ev.get(("c", it - 1))],
                                        inc=sA)
                            ev[("c", it)] = m1
                            ev[("silu", hb)] = m1
                        else:
                            m2 = R.emit("vector", "scalar_tensor_tensor",
                                        out=fhg[:, hb * TF:(hb + 1) * TF],
                                        in0=pbk[bk][:, :],
                                        scalar=p_f2s[:, hb:hb + 1],
                                        in1=ftmp[:, :], op0=ALU.mult,
                                        op1=ALU.mult,
                                        waits=[mmf, ev[("silu", hb)]], inc=sV)
                            ev[("c", it)] = m2
                mhg = R.now(sV)
                for hop, (wsrc, sclt, src_buf) in enumerate(
                        ((w_h1T, p_h1s, fhg), (w_h2T, p_h2s, fhg2))):
                    hr_m = [None, None]
                    for hb in range(NHB):
                        sl_i = hb % 2
                        hr_m[sl_i] = R.emit(
                            "gpsimd", "dma_start",
                            out=fwh[:, sl_i * 4096:(sl_i + 1) * 4096]
                                .rearrange("p (k e) -> p k e", k=NHB),
                            in_=wsrc[:, hb * 128:(hb + 1) * 128]
                                .rearrange("(k p) e -> p k e", p=128),
                            waits=[ev.get(("h", hop, hb - 2), R.now(sP))],
                            inc=dG, incn=16)
                        bk = hb % 2
                        mmf = None
                        for k in range(NHB):
                            mmf = R.emit(
                                "tensor", "matmul", pbk[bk][:, :],
                                lhsT=fwh[:, sl_i * 4096 + k * 128:
                                         sl_i * 4096 + (k + 1) * 128],
                                rhs=src_buf[:, k * TF:(k + 1) * TF],
                                start=(k == 0), stop=(k == NHB - 1),
                                waits=([mhg, hr_m[sl_i],
                                        ev.get(("h", hop, hb - 2))]
                                       if k == 0 else ()),
                                inc=sP if k == NHB - 1 else None)
                        if hop == 0:
                            mg = R.emit("scalar", "activation",
                                        out=fhg2[:, hb * TF:(hb + 1) * TF],
                                        in_=pbk[bk][:, :], func=AF.Gelu,
                                        scale=p_h1s[:, hb:hb + 1],
                                        waits=[mmf], inc=sA)
                            ev[("h", hop, hb)] = mg
                        else:
                            st_i = hb % 2
                            mg = R.emit("scalar", "activation",
                                        out=fst[:, st_i * TF:(st_i + 1) * TF],
                                        in_=pbk[bk][:, :], func=AF.Gelu,
                                        scale=p_h2s[:, hb:hb + 1],
                                        waits=[mmf, ev.get(("hst", hb - 2))],
                                        inc=sA)
                            ev[("h", hop, hb)] = mg
                            ev[("hst", hb)] = R.emit(
                                "gpsimd", "dma_start", out=sp_hg3[hb, :, fl],
                                in_=fst[:, st_i * TF:(st_i + 1) * TF],
                                waits=[mg], inc=dG, incn=16)
                    if hop == 0:
                        mhg = R.now(sA)
            R.barrier()
        if s == 0 and dbg:
            dbg_dump("hg3", sp_hg3[:, :, :], (NHB, 128, L), BF16, 2)
            dbg_dump("rfin", sp_r[:, :, :], (NDB, 128, L), F32, 2)

        R.stage = "FIN"
        with ExitStack() as stG:
            g_hg = sb("g_hg", (128, NHB * TF), BF16, stG)
            g_mo = sb("g_mo", (128, NDB * TF), F32, stG)
            g_x = sb("g_x", (128, NDB * TF), F32, stG)
            g_o = sb("g_o", (128, NDB * TF), F32, stG)
            ring_f3 = sb("ring_f3", (128, 2 * NHB * 128), BF16, stG)
            f3ev = {}
            fr3 = [None, None]
            for ft in range(NFT):
                fl = slice(ft * TF, (ft + 1) * TF)
                wpre = [R.now(sP), R.now(sV)]
                mhl = R.emit("gpsimd", "dma_start",
                             out=g_hg.ap().rearrange("p (k t) -> p k t", k=NHB),
                             in_=sp_hg3[:, :, fl].rearrange("k p t -> p k t"), waits=wpre, inc=dG, incn=16)
                mrl = R.emit("sync", "dma_start",
                             out=g_x.ap().rearrange("p (k t) -> p k t", k=NDB),
                             in_=sp_r[:, :, fl].rearrange("b p t -> p b t"), waits=wpre, inc=dS, incn=16)
                lastln = []
                for half in range(TF // TC):
                    off = half * TC
                    src_i = (lambda i, off=off:
                             g_x[:, i * TF + off:i * TF + off + TC])
                    out_i = (lambda i, off=off:
                             g_mo[:, i * TF + off:i * TF + off + TC])
                    la = ln_stats(src_i, NDB, 2, 3, 4,
                                  [mrl, R.now(sV)] + lastln, out_i,
                                  affine=(lambda i: p_nfw[:, i:i + 1],
                                          lambda i: p_nfb[:, i:i + 1]))
                    lastln.append(la)
                mxl = R.emit("sync", "dma_start",
                             out=g_x.ap().rearrange("p (k t) -> p k t", k=NDB),
                             in_=xs_t[:, :, fl].rearrange("b p t -> p b t"), waits=lastln, inc=dS, incn=16)
                for odb in range(NDB):
                    sl_i = odb % 2
                    fr3[sl_i] = R.emit(
                        "gpsimd", "dma_start",
                        out=ring_f3[:, sl_i * 4096:(sl_i + 1) * 4096]
                            .rearrange("p (k e) -> p k e", k=NHB),
                        in_=w_f3T[:, odb * 128:(odb + 1) * 128]
                            .rearrange("(k p) e -> p k e", p=128),
                        waits=[f3ev.get((ft, odb - 2))], inc=dG, incn=16)
                    bk = odb % 2
                    mmf = None
                    for k in range(NHB):
                        mmf = R.emit(
                            "tensor", "matmul", pbk[bk][:, :],
                            lhsT=ring_f3[:, sl_i * 4096 + k * 128:
                                         sl_i * 4096 + (k + 1) * 128],
                            rhs=g_hg[:, k * TF:(k + 1) * TF],
                            start=(k == 0), stop=(k == NHB - 1),
                            waits=([mhl, fr3[sl_i], f3ev.get((ft, odb - 2)),
                                    f3ev.get((ft - 1, NDB - 2 + odb % 2))]
                                   if k == 0 else ()),
                            inc=sP if k == NHB - 1 else None)
                    m1 = R.emit("vector", "scalar_tensor_tensor",
                                out=g_o[:, odb * TF:(odb + 1) * TF],
                                in0=pbk[bk][:, :], scalar=p_f3s[:, odb:odb + 1],
                                in1=g_mo[:, odb * TF:(odb + 1) * TF],
                                op0=ALU.mult, op1=ALU.add,
                                waits=[mmf] + lastln, inc=sV)
                    f3ev[(ft, odb)] = R.emit(
                        "vector", "tensor_tensor",
                        out=g_o[:, odb * TF:(odb + 1) * TF],
                        in0=g_o[:, odb * TF:(odb + 1) * TF],
                        in1=g_x[:, odb * TF:(odb + 1) * TF], op=ALU.add,
                        waits=[mxl, m1], inc=sV)
                R.emit("sync", "dma_start",
                       out=out_fm[s].rearrange("(k p) t -> k p t", p=128)
                           [:, :, fl].rearrange("k p t -> p k t"),
                       in_=g_o.ap().rearrange("p (k t) -> p k t", k=NDB),
                       waits=[f3ev[(ft, o)] for o in range(NDB)],
                       inc=dS, incn=16)
            R.barrier()

    R.finish()

    with nc.Block() as blk:
        blk.tensor(lambda t: [f(t) for f in R.sched["tensor"]])
        blk.vector(lambda v: [f(v) for f in R.sched["vector"]])
        blk.scalar(lambda sc: [f(sc) for f in R.sched["scalar"]])
        blk.gpsimd(lambda g: [f(g) for f in R.sched["gpsimd"]])
        blk.sync(lambda sy: [f(sy) for f in R.sched["sync"]])

    es.close()
    return nc


def kernel(**inputs):
    host = _prep_host(inputs)
    nc = build_program()
    in_maps = []
    for c in range(NCORES):
        m = dict(host)
        m["x_fm"] = np.ascontiguousarray(host["x_fm"][c * SPC:(c + 1) * SPC])
        in_maps.append(m)
    res = run_bass_kernel_spmd(nc, in_maps, core_ids=list(range(NCORES)))
    full = np.concatenate([r["out_fm"] for r in res.results], axis=0)
    return np.ascontiguousarray(full.transpose(0, 2, 1)).astype(np.float32)
